# version 12 (final)
# baseline (speedup 1.0000x reference)
import numpy as np
import ml_dtypes
from contextlib import ExitStack

import concourse.bass as bass
import concourse.mybir as mybir
from concourse.bass_utils import run_bass_kernel_spmd

F32 = mybir.dt.float32
BF16 = mybir.dt.bfloat16
AF = mybir.ActivationFunctionType
ALU = mybir.AluOpType
AX = mybir.AxisListType
BF = ml_dtypes.bfloat16

B, S, DM = 4, 2048, 2048
NHL, HD, GRP, KC = 8, 128, 2, 16
TOK = 1024
NR, H, HC = 6, 1024, 8
SCALE = HD ** -0.5
EPS = 1e-5

ENGINES = ("sync", "gpsimd", "tensor", "vector", "scalar")
SEMS = ("LDW", "LDS", "STS", "MM", "VE", "SC", "CC", "LDA")


class Prog:
    def __init__(self):
        self.steps = {e: [] for e in ENGINES}
        self.cnt = {s: 0 for s in SEMS}
        self.ensured = {e: {s: 0 for s in SEMS} for e in ENGINES}

    def emit(self, eng, fn, waits=(), inc=None, n=1):
        need = []
        for s, v in waits:
            if v is None:
                continue
            assert v <= self.cnt[s], f"wait {s}>={v} > emitted {self.cnt[s]}"
            if v > self.ensured[eng][s]:
                need.append((s, v))
                self.ensured[eng][s] = v
        if inc is not None:
            self.cnt[inc] += n
        self.steps[eng].append((tuple(need), fn, inc, n))
        return self.cnt[inc] if inc is not None else None


class View:

    def __init__(self, arena, off, n, w):
        self.a, self.off, self.n, self.w = arena, off, n, w

    def __getitem__(self, i):
        assert 0 <= i < self.n
        if not hasattr(self, "_c"):
            self._c = {}
        if i not in self._c:
            o = self.off + i * self.w
            self._c[i] = self.a[:, o:o + self.w]
        return self._c[i]


def build_nc():
    nc = bass.Bass("TRN2", target_bir_lowering=False)
    d = {}

    def inp(name, shape, dt=F32):
        d[name] = nc.dram_tensor(name, list(shape), dt, kind="ExternalInput").ap()
        return d[name]

    xT = inp("xT", (DM, S))
    xownT = inp("xownT", (DM, TOK))
    wq_img = inp("wq_img", (4, 128, 4096))
    wk_img = inp("wk_img", (4, 128, 4096))
    wv_img = inp("wv_img", (4, 128, 4096))
    woTl = inp("woTl", (DM, DM))
    bqc = inp("bqc", (128, NHL))
    bkc = inp("bkc", (128, NHL))
    bvr = inp("bvr", (1, NHL * HD))
    boc = inp("boc", (128, KC))
    Wg2 = inp("Wg2", (128, 128))
    bgr = inp("bgr", (8, 1))
    Wg_img = inp("Wg_img", (NR, 4, 128, 4096))
    Wu_img = inp("Wu_img", (NR, 4, 128, 4096))
    bgatec = inp("bgatec", (128, NR * HC))
    bupc = inp("bupc", (128, NR * HC))
    Wdn_img = inp("Wdn_img", (8, NR, 128, 2048))
    bdn = inp("bdn", (NR, DM))
    Ws_img = inp("Ws_img", (8, 128, 4096))
    bsupc = inp("bsupc", (128, KC))
    Wsdn_img = inp("Wsdn_img", (8, 128, 4096))
    bsdnc = inp("bsdnc", (128, KC))
    cosq_d = inp("cosq", (128, S), BF16)
    sinq_d = inp("sinq", (128, S), BF16)
    cosk_d = inp("cosk", (128, S), BF16)
    sink_d = inp("sink", (128, S), BF16)
    dmask = inp("dmask", (128, 4 * 512), BF16)
    idf = inp("idf", (128, 128))
    selc = inp("selc", (128, 4))
    selh = inp("selh", (NR, NR * 128))
    rsT_d = inp("rsT", (128, 128), BF16)

    out = nc.dram_tensor("out", [DM, TOK], F32, kind="ExternalOutput").ap()

    cc_in_a = nc.dram_tensor("cc_in_a", [4 * HD, TOK], BF16).ap()
    cc_out_a = nc.dram_tensor("cc_out_a", [8 * HD, TOK], BF16).ap()
    cc_in_b = nc.dram_tensor("cc_in_b", [4 * HD, TOK], BF16).ap()
    cc_out_b = nc.dram_tensor("cc_out_b", [8 * HD, TOK], BF16).ap()
    x1_dram = nc.dram_tensor("x1_dram", [DM, TOK], F32).ap()
    xln_dram = nc.dram_tensor("xln_dram", [DM, TOK], F32).ap()
    x2_dram = nc.dram_tensor("x2_dram", [DM, TOK], F32).ap()
    sep_dram = nc.dram_tensor("sep_dram", [1, 64], F32).ap()

    p = Prog()
    E = p.emit
    es = ExitStack()

    def sbt(name, shape, dt):
        return es.enter_context(nc.sbuf_tensor(name, list(shape), dt))

    with es:
        AXa = sbt("AXa", (128, 32768), BF16)
        AYa = sbt("AYa", (128, 16384), BF16)
        AWa = sbt("AWa", (128, 11264), BF16)
        AVa = sbt("AVa", (128, 4096), BF16)
        AFa = sbt("AFa", (128, 7168), F32)

        xs = View(AXa, 0, KC, S)
        wo_slab = View(AXa, 0, KC, DM)
        xln16 = View(AXa, 0, KC, TOK)
        partial = View(AXa, KC * TOK, KC, TOK)
        qrot = View(AYa, 0, GRP, S)
        krot = View(AYa, GRP * S, GRP, S)
        vtm = View(AYa, 2 * GRP * S, KC, 256)
        wslab = View(AYa, 2 * GRP * S + KC * 256, KC, 256)
        myown = View(AYa, 0, NHL, TOK)
        partner = View(AYa, NHL * TOK, NHL, TOK)
        hsf = View(AYa, 0, 32, 512)
        c_cosq = View(AWa, 0, 1, S)[0]
        c_sinq = View(AWa, S, 1, S)[0]
        c_cosk = View(AWa, 2 * S, 1, S)[0]
        c_sink = View(AWa, 3 * S, 1, S)[0]
        maskt = View(AWa, 4 * S, 4, 512)
        rcv = View(AWa, 0, 4, TOK)
        sendt = View(AWa, 4 * TOK, 2, TOK)
        gslab = View(AWa, 0, 2, 4096)
        sg = View(AWa, 8192, 4, 512)
        ut = View(AWa, 8192 + 4 * 512, 2, 512)
        stw = View(AWa, 0, 2, 4096)
        probs = View(AVa, 0, 3, 512)
        sq512 = View(AVa, 3 * 512, 1, 512)[0]
        rope16 = View(AVa, 2048, 2, 512)
        sendE = View(AVa, 3072, 1, 1024)[0]
        rtw = View(AVa, 0, 2, HC * 256)
        ropet = View(AFa, 0, 3, 512)
        rB = View(AFa, 3 * 512, 1, 512)[0]
        ssum_t = AFa[0:1, 2048:2560]
        xot = View(AFa, 0, 2, 512)
        x1t = View(AFa, 2 * 512, 2, 512)
        sqf = View(AFa, 4 * 512, 2, 512)
        xlnt = View(AFa, 6 * 512, 2, 512)
        wBt = View(AFa, 0, 2, 512)
        x2t = View(AFa, 2 * 512, 2, 512)
        x3t = View(AFa, 6 * 512, 2, 512)
        xlt = View(AFa, 6 * 512, 2, 512)
        nm_r = AFa[0:1, 4096:5120]
        inv_r = AFa[0:1, 5120:6144]
        tmp_r = AFa[0:1, 6144:7168]
        gt_sb = AFa[0:8, 6144:7168]

        attnfm = sbt("attnfm", (128, NHL * S), BF16)
        atv = View(attnfm, 0, NHL, S)
        idf_s = sbt("idf_s", (128, 128), F32)
        selc_s = sbt("selc_s", (128, 4), F32)
        bq_s = sbt("bq_s", (128, NHL), F32)
        bk_s = sbt("bk_s", (128, NHL), F32)
        bv_s = sbt("bv_s", (1, NHL * HD), BF16)
        bo_s = sbt("bo_s", (128, KC), F32)
        wg_s = sbt("wg_s", (128, 128), F32)
        bg_s = sbt("bg_s", (8, 1), F32)
        bgate_s = sbt("bgate_s", (128, NR * HC), F32)
        bup_s = sbt("bup_s", (128, NR * HC), F32)
        bdn_s = sbt("bdn_s", (NR, DM), BF16)
        wT6b = sbt("wT6b", (NR, TOK), BF16)
        bsup_s = sbt("bsup_s", (128, KC), F32)
        bsdn_s = sbt("bsdn_s", (128, KC), F32)
        ones_b = sbt("ones_b", (128, 1), BF16)
        onesr_b = sbt("onesr_b", (1, 128), BF16)
        onesr_f = sbt("onesr_f", (1, 128), F32)
        onesc_f = sbt("onesc_f", (128, 1), F32)
        scr = sbt("scr", (128, 512), F32)
        gate_s = sbt("gate_s", (128, 64), F32)
        w_s = sbt("w_s", (128, 64), F32)
        w128 = sbt("w128", (128, 128), F32)
        wT6 = sbt("wT6", (NR, TOK), F32)
        negb = sbt("negb", (128, 1), F32)
        selh_s = sbt("selh_s", (NR, NR * 128), F32)
        rsT_s = sbt("rsT_s", (128, 128), BF16)
        eps_s = sbt("eps_s", (1, 1), F32)
        sep16 = sbt("sep16", (128, 16), BF16)
        sepd_s = sbt("sepd_s", (1, 64), F32)

        P = [es.enter_context(nc.psum_tensor(f"P{i}", [128, 512], F32))
             for i in range(8)]
        sem = {s: es.enter_context(nc.semaphore(s)) for s in SEMS}
        block = es.enter_context(nc.Block())

        pfree = {}

        def key(ap_or_obj):
            return id(ap_or_obj)

        def pw(*objs):
            w = []
            for t in objs:
                w.extend(pfree.get(key(t), []))
            return w

        def setfree(obj, semname, val):
            pfree[key(obj)] = [(semname, val)]

        def addfree(obj, semname, val):
            pfree.setdefault(key(obj), []).append((semname, val))

        def vsep():
            E("vector", lambda e:
              e.tensor_copy(sep16[:, 0:8], sep16[:, 8:16]))

        def vfence():
            vsep()
            vsep()
            return E("vector", lambda e:
                     e.tensor_copy(sep16[:, 0:8], sep16[:, 8:16]), inc="VE")

        def sfence():
            E("scalar", lambda e:
              e.activation(sep16[:, 0:8], sep16[:, 8:16], AF.Copy))
            return E("scalar", lambda e:
                     e.activation(sep16[:, 0:8], sep16[:, 8:16], AF.Copy),
                     inc="SC")

        def dma(eng, sm, outap, inap, waits=(), echo=False):
            r = E(eng, lambda e, o=outap, i=inap: e.dma_start(out=o, in_=i),
                  waits=waits, inc=sm, n=16)
            if echo:
                r = E(eng, lambda e: e.dma_start(out=sepd_s[:, :],
                                                 in_=sep_dram[:, :]),
                      inc=sm, n=16)
            return r

        for t_, src in ((c_cosq, cosq_d), (c_sinq, sinq_d), (c_cosk, cosk_d),
                        (c_sink, sink_d), (idf_s[:, :], idf),
                        (selc_s[:, :], selc), (bq_s[:, :], bqc),
                        (bk_s[:, :], bkc), (bo_s[:, :], boc),
                        (wg_s[:, :], Wg2), (bg_s[:, :], bgr),
                        (bgate_s[:, :], bgatec), (bup_s[:, :], bupc),
                        (bsup_s[:, :], bsupc),
                        (selh_s[:, :], selh), (rsT_s[:, :], rsT_d),
                        (bsdn_s[:, :], bsdnc)):
            dma("sync", "LDS", t_, src[:, :])
        for m in range(4):
            dma("sync", "LDS", maskt[m], dmask[:, m * 512:(m + 1) * 512])
        ld_consts = dma("sync", "LDS", sepd_s[:, :], sep_dram[:, :],
                        echo=True)
        ld_bv = dma("gpsimd", "LDW", bv_s[:, :], bvr[:, :], echo=True)
        ld_bdn = dma("gpsimd", "LDW", bdn_s[:, :], bdn[:, :], echo=True)

        ms_final = None
        for ap_, v in ((ones_b[:, :], 1.0), (onesr_b[:, :], 1.0),
                       (onesr_f[:, :], 1.0), (onesc_f[:, :], 1.0),
                       (eps_s[:, :], float(EPS)),
                       (negb[:, :], -20.0),
                       (w128[:, :], 0.0)):
            ms_final = E("vector", lambda e, a=ap_, vv=v: e.memset(a, vv),
                         inc="VE")
        ms_final = vfence()

        ldx = {}
        for kc in range(KC):
            ldx[kc] = dma("gpsimd", "LDW", xs[kc],
                          xT[kc * 128:(kc + 1) * 128, :], echo=True)

        for g in range(4):
            cols = slice(g * 256, (g + 1) * 256)
            for wsrc in ("q", "k", "v"):
                wd = {"q": wq_img, "k": wk_img, "v": wv_img}[wsrc]
                ldslab = dma("gpsimd", "LDW",
                             AYa[:, wslab.off:wslab.off + 4096], wd[g],
                             waits=pw(wslab), echo=True)
                if wsrc in ("q", "k"):
                    bias = bq_s if wsrc == "q" else bk_s
                    cos = c_cosq if wsrc == "q" else c_cosk
                    sinm = c_sinq if wsrc == "q" else c_sink
                    dst = qrot if wsrc == "q" else krot
                    for hl in range(GRP):
                        h = g * GRP + hl
                        for sl in range(4):
                            pt = P[(0, 1, 5, 6)[sl % 4]]
                            mm = None
                            for kc in range(KC):
                                mm = E("tensor",
                                       lambda e, pt=pt, kc=kc, hl=hl, sl=sl:
                                       e.matmul(
                                           pt[:, :],
                                           wslab[kc][:,
                                                     hl * 128:(hl + 1) * 128],
                                           xs[kc][:, sl * 512:(sl + 1) * 512],
                                           start=(kc == 0), stop=(kc == 15)),
                                       waits=([("LDW", ldslab),
                                               ("LDW", ldx[kc])]
                                              + (pw(pt) if kc == 0 else [])),
                                       inc="MM" if kc == 15 else None)
                            dsl = slice(sl * 512, (sl + 1) * 512)
                            qb = rope16[sl % 2]
                            ve = E("vector",
                                   lambda e, pt=pt, h=h, bias=bias, qb=qb:
                                   e.tensor_scalar(qb, pt[:, :],
                                                   bias[:, h:h + 1], None,
                                                   op0=ALU.add),
                                   waits=[("MM", mm), ("LDS", ld_consts)]
                                         + pw(qb), inc="VE")
                            setfree(pt, "VE", ve)
                            prot = P[2 + (sl % 2)]
                            mmr = E("tensor", lambda e, prot=prot, qb=qb:
                                    e.matmul(prot[:, :], rsT_s[:, :], qb,
                                             start=True, stop=True),
                                    waits=[("VE", ve)] + pw(prot), inc="MM")
                            addfree(qb, "MM", mmr)
                            ve2 = E("vector",
                                    lambda e, dsl=dsl, sinm=sinm, prot=prot:
                                    e.tensor_tensor(ropet[1], prot[:, :],
                                                    sinm[:, dsl],
                                                    op=ALU.mult),
                                    waits=[("MM", mmr)], inc="VE")
                            setfree(prot, "VE", ve2)
                            E("vector", lambda e, dsl=dsl, cos=cos, qb=qb:
                              e.tensor_tensor(ropet[0], qb,
                                              cos[:, dsl], op=ALU.mult))
                            vsep()
                            ve = E("vector",
                                   lambda e, hl=hl, dsl=dsl, dst=dst:
                                   e.tensor_tensor(dst[hl][:, dsl], ropet[0],
                                                   ropet[1], op=ALU.add),
                                   inc="VE")
                            addfree(qb, "VE", ve)
                else:
                    for tcv in range(KC):
                        pt = P[(0, 1, 5, 6)[tcv % 4]]
                        for kc in range(KC):
                            E("tensor", lambda e, pt=pt, kc=kc, tcv=tcv:
                              e.matmul(pt[:, 0:256],
                                       xs[kc][:, tcv * 128:(tcv + 1) * 128],
                                       wslab[kc],
                                       start=(kc == 0), stop=False),
                              waits=([("LDW", ldslab), ("LDW", ldx[kc])]
                                     + (pw(pt) if kc == 0 else [])),
                              inc=None)
                        mm = E("tensor", lambda e, pt=pt, g=g:
                               e.matmul(pt[:, 0:256], onesr_b[0:1, :],
                                        bv_s[0:1, g * 256:(g + 1) * 256],
                                        start=False, stop=True),
                               waits=[("VE", ms_final), ("LDW", ld_bv)],
                               inc="MM")
                        ve = E("vector", lambda e, pt=pt, tcv=tcv:
                               e.tensor_copy(vtm[tcv], pt[:, 0:256]),
                               waits=[("MM", mm)], inc="VE")
                        setfree(pt, "VE", ve)
                setfree(wslab, "MM", p.cnt["MM"])
            ve_qkv = vfence()
            if g == 3:
                mm_xs_free = p.cnt["MM"]

            for hl in range(GRP):
                h = g * GRP + hl
                for qs in range(4):
                    kcount = 4 * (qs + 1)
                    ppv = P[(0, 1, 2)[(g * GRP * 4 + hl * 4 + qs) % 3]]
                    psm = P[6 + ((g * GRP * 4 + hl * 4 + qs) % 2)]
                    qsl = slice(qs * 512, (qs + 1) * 512)
                    for kb in range(kcount):
                        pt = P[(3, 4)[kb % 2]]
                        mm = E("tensor", lambda e, pt=pt, hl=hl, kb=kb,
                               qsl=qsl:
                               e.matmul(pt[:, :],
                                        krot[hl][:, kb * 128:(kb + 1) * 128],
                                        qrot[hl][:, qsl],
                                        start=True, stop=True),
                               waits=[("VE", ve_qkv)] + pw(pt), inc="MM")
                        pr = probs[kb % 3]
                        sc_ = E("scalar", lambda e, pt=pt, pr=pr:
                                e.activation(pr, pt[:, :], AF.Exp,
                                             bias=negb[:, 0:1],
                                             scale=1.0),
                                waits=[("MM", mm), ("VE", ms_final)] + pw(pr),
                                inc="SC")
                        setfree(pt, "SC", sc_)
                        dep = ("SC", sc_)
                        m = kb - 4 * qs
                        if m >= 0:
                            vv = E("vector", lambda e, pr=pr, m=m:
                                   e.tensor_tensor(pr, pr, maskt[m],
                                                   op=ALU.mult),
                                   waits=[dep, ("LDS", ld_consts)], inc="VE")
                            dep = ("VE", vv)
                        first = (kb == 0)
                        last = (kb == kcount - 1)
                        E("tensor", lambda e, ppv=ppv, kb=kb, hl=hl, pr=pr,
                          first=first, last=last:
                          e.matmul(ppv[:, :],
                                   vtm[kb][:, hl * 128:(hl + 1) * 128],
                                   pr, start=first, stop=last),
                          waits=[dep] + (pw(ppv) if first else []),
                          inc="MM" if last else None)
                        E("tensor", lambda e, psm=psm, pr=pr,
                          first=first, last=last:
                          e.matmul(psm[0:1, :], ones_b[:, 0:1],
                                   pr, start=first, stop=last),
                          waits=(pw(psm) if first else []),
                          inc="MM" if last else None)
                        setfree(pr, "MM", p.cnt["MM"])
                    mm_all = p.cnt["MM"]
                    ve = E("vector", lambda e, psm=psm:
                           e.tensor_copy(ssum_t, psm[0:1, :]),
                           waits=[("MM", mm_all)] + pw(ssum_t), inc="VE")
                    setfree(psm, "VE", ve)
                    ve = E("vector", lambda e:
                           e.reciprocal(ssum_t, ssum_t), inc="VE")
                    pb = P[5]
                    mm = E("tensor", lambda e, pb=pb:
                           e.matmul(pb[:, :], onesr_f[0:1, :],
                                    ssum_t, start=True, stop=True),
                           waits=[("VE", ve)] + pw(pb), inc="MM")
                    setfree(ssum_t, "MM", mm)
                    ve = E("vector", lambda e, pb=pb:
                           e.tensor_copy(rB, pb[:, :]),
                           waits=[("MM", mm)], inc="VE")
                    setfree(pb, "VE", ve)
                    ve = E("vector", lambda e, h=h, qsl=qsl, ppv=ppv:
                           e.tensor_tensor(atv[h][:, qsl], ppv[:, :],
                                           rB, op=ALU.mult), inc="VE")
                    setfree(ppv, "VE", ve)

            if g in (1, 3):
                half = g // 2
                ccin = cc_in_a if half == 0 else cc_in_b
                for cl in range(4):
                    c = half * 4 + cl
                    sd = sendE
                    E("vector", lambda e, c=c, sd=sd:
                      e.tensor_scalar(sd, atv[c][:, 0:TOK],
                                      selc_s[:, 0:1], None, op0=ALU.mult),
                      waits=[("LDS", ld_consts)] + pw(sd))
                    vsep()
                    ve = E("vector", lambda e, c=c, sd=sd:
                           e.scalar_tensor_tensor(sd, atv[c][:, TOK:S],
                                                  selc_s[:, 1:2], sd,
                                                  op0=ALU.mult, op1=ALU.add),
                           inc="VE")
                    st_ = dma("sync", "STS",
                              ccin[cl * 128:(cl + 1) * 128, :], sd,
                              waits=[("VE", ve)])
                    setfree(sd, "STS", st_)
                if half == 0:
                    sts_send_a = p.cnt["STS"]
                else:
                    sts_send_b = p.cnt["STS"]

        mm_endB = p.cnt["MM"]
        ve_attn = p.cnt["VE"]
        sc_endB = p.cnt["SC"]

        for c in range(NHL):
            E("vector", lambda e, c=c:
              e.tensor_scalar(myown[c], atv[c][:, 0:TOK],
                              selc_s[:, 2:3], None, op0=ALU.mult),
              waits=[("VE", ve_attn), ("LDS", ld_consts), ("MM", mm_endB),
                     ("SC", sc_endB)])
        for c in range(NHL):
            E("vector", lambda e, c=c:
              e.scalar_tensor_tensor(myown[c], atv[c][:, TOK:S],
                                     selc_s[:, 3:4], myown[c],
                                     op0=ALU.mult, op1=ALU.add),
              inc="VE")
        for kc in range(KC):
            dma("gpsimd", "LDW", wo_slab[kc],
                woTl[kc * 128:(kc + 1) * 128, :],
                waits=[("MM", mm_xs_free)],
                echo=(kc == KC - 1))
        ld_wo = p.cnt["LDW"]
        E("gpsimd", None, waits=[("LDW", ld_wo)])
        cc = E("gpsimd", lambda e:
               e.collective_compute(
                   "AllGather", ALU.bypass,
                   replica_groups=[[0, 1], [2, 3], [4, 5], [6, 7]],
                   ins=[cc_in_a[:, :]], outs=[cc_out_a[:, :]]),
               waits=[("STS", sts_send_a)], inc="CC")
        cc = E("gpsimd", lambda e:
               e.collective_compute(
                   "AllGather", ALU.bypass,
                   replica_groups=[[0, 1], [2, 3], [4, 5], [6, 7]],
                   ins=[cc_in_b[:, :]], outs=[cc_out_b[:, :]]),
               waits=[("STS", sts_send_b)], inc="CC")
        for c in range(NHL):
            half = c // 4
            cl = c % 4
            ccout = cc_out_a if half == 0 else cc_out_b
            r0 = rcv[(c % 2) * 2]
            r1 = rcv[(c % 2) * 2 + 1]
            l0 = dma("sync", "LDS", r0, ccout[cl * 128:(cl + 1) * 128, :],
                     waits=[("CC", 1 + half), ("VE", ve_attn)] + pw(r0),
                     echo=True)
            l1 = dma("sync", "LDS", r1,
                     ccout[(4 + cl) * 128:(4 + cl + 1) * 128, :],
                     waits=pw(r1), echo=True)
            E("vector", lambda e, c=c, r0=r0:
              e.tensor_scalar(partner[c], r0, selc_s[:, 0:1], None,
                              op0=ALU.mult),
              waits=[("LDS", l0)])
            vsep()
            ve = E("vector", lambda e, c=c, r1=r1:
                   e.scalar_tensor_tensor(partner[c], r1, selc_s[:, 1:2],
                                          partner[c], op0=ALU.mult,
                                          op1=ALU.add),
                   waits=[("LDS", l1)], inc="VE")
            setfree(r0, "VE", ve)
            setfree(r1, "VE", ve)
        ve_part = vfence()

        x1_store = {}
        for ts in range(2):
            tsl = slice(ts * 512, (ts + 1) * 512)
            for oc in range(KC):
                pt = P[(0, 1, 2, 4)[oc % 4]]
                mm = None
                for kc in range(KC):
                    rhs = myown[kc] if kc < 8 else partner[kc - 8]
                    mm = E("tensor", lambda e, pt=pt, kc=kc, oc=oc, rhs=rhs,
                           tsl=tsl:
                           e.matmul(pt[:, :],
                                    wo_slab[kc][:, oc * 128:(oc + 1) * 128],
                                    rhs[:, tsl],
                                    start=(kc == 0), stop=(kc == 15)),
                           waits=([("LDW", ld_wo), ("VE", ve_part)]
                                  + (pw(pt) if kc == 0 else [])),
                           inc="MM" if kc == 15 else None)
                xo = xot[oc % 2]
                lx = dma("scalar", "LDA", xo,
                         xownT[oc * 128:(oc + 1) * 128, tsl],
                         waits=[("MM", mm_endB), ("VE", ve_attn)] + pw(xo),
                         echo=True)
                xt_ = x1t[oc % 2]
                ve = E("vector", lambda e, pt=pt, oc=oc, xo=xo, xt_=xt_:
                       e.scalar_tensor_tensor(xt_, pt[:, :],
                                              bo_s[:, oc:oc + 1], xo,
                                              op0=ALU.add, op1=ALU.add),
                       waits=[("MM", mm), ("LDA", lx)] + pw(xt_), inc="VE")
                setfree(pt, "VE", ve)
                setfree(xo, "VE", ve)
                vv = vfence()
                st_ = dma("sync", "STS",
                          x1_dram[oc * 128:(oc + 1) * 128, tsl], xt_,
                          waits=[("VE", vv)])
                x1_store[(oc, ts)] = st_
                sq = sqf[oc % 2]
                sc_ = E("scalar", lambda e, xt_=xt_, sq=sq:
                        e.activation(sq, xt_, AF.Square),
                        waits=[("VE", vv)] + pw(sq), inc="SC")
                sc2 = sfence()
                mm1 = E("tensor", lambda e, xt_=xt_, oc=oc:
                        e.matmul(P[6][0:1, :], onesc_f[:, 0:1], xt_,
                                 start=(oc == 0), stop=(oc == 15)),
                        waits=([("VE", vv)]
                               + (pw(P[6]) if oc == 0 else [])),
                        inc="MM")
                setfree(xt_, "MM", mm1)
                addfree(xt_, "SC", sc_)
                addfree(xt_, "STS", st_)
                mm2 = E("tensor", lambda e, sq=sq, oc=oc:
                        e.matmul(P[7][0:1, :], onesc_f[:, 0:1], sq,
                                 start=(oc == 0), stop=(oc == 15)),
                        waits=([("SC", sc2)]
                               + (pw(P[7]) if oc == 0 else [])),
                        inc="MM")
                setfree(sq, "MM", mm2)
            ve = E("vector", lambda e, ts=ts:
                   e.tensor_copy(nm_r[:, ts * 512:(ts + 1) * 512],
                                 P[6][0:1, :]),
                   waits=[("MM", p.cnt["MM"])], inc="VE")
            setfree(P[6], "VE", ve)
            ve = E("vector", lambda e, ts=ts:
                   e.tensor_copy(inv_r[:, ts * 512:(ts + 1) * 512],
                                 P[7][0:1, :]), inc="VE")
            setfree(P[7], "VE", ve)

        def ln_stats():
            E("vector", lambda e:
              e.tensor_scalar(nm_r, nm_r, -1.0 / DM, None, op0=ALU.mult))
            E("vector", lambda e:
              e.tensor_scalar(inv_r, inv_r, 1.0 / DM, None, op0=ALU.mult))
            E("vector", lambda e:
              e.tensor_tensor(tmp_r, nm_r, nm_r, op=ALU.mult))
            ve = E("vector", lambda e:
                   e.scalar_tensor_tensor(tmp_r, tmp_r, -1.0, inv_r,
                                          op0=ALU.mult, op1=ALU.add),
                   inc="VE")
            sc_ = E("scalar", lambda e:
                    e.activation(tmp_r, tmp_r, AF.Sqrt,
                                 bias=eps_s[0:1, 0:1], scale=1.0),
                    waits=[("VE", ve)], inc="SC")
            E("vector", lambda e:
              e.reciprocal(inv_r, tmp_r),
              waits=[("SC", sc_)], inc="VE")
            return vfence()

        ve_st1 = ln_stats()
        for ts in range(2):
            tsl = slice(ts * 512, (ts + 1) * 512)
            pnm, pinv = P[5], P[3]
            E("tensor", lambda e, pnm=pnm, ts=ts:
              e.matmul(pnm[:, :], onesr_f[0:1, :],
                       nm_r[:, ts * 512:(ts + 1) * 512],
                       start=True, stop=True),
              waits=[("VE", ve_st1)] + pw(pnm), inc="MM")
            mmb = E("tensor", lambda e, pinv=pinv, ts=ts:
                    e.matmul(pinv[:, :], onesr_f[0:1, :],
                             inv_r[:, ts * 512:(ts + 1) * 512],
                             start=True, stop=True),
                    waits=pw(pinv), inc="MM")
            for oc in range(KC):
                xt_ = x1t[oc % 2]
                lx = dma("scalar", "LDA", xt_,
                         x1_dram[oc * 128:(oc + 1) * 128, tsl],
                         waits=[("STS", x1_store[(oc, ts)])] + pw(xt_),
                         echo=True)
                xl = xlnt[oc % 2]
                E("vector", lambda e, xt_=xt_, xl=xl, pnm=pnm:
                  e.tensor_tensor(xl, xt_, pnm[:, :], op=ALU.add),
                  waits=[("LDA", lx), ("MM", mmb)] + pw(xl))
                vsep()
                ve = E("vector", lambda e, xl=xl, pinv=pinv:
                       e.tensor_tensor(xl, xl, pinv[:, :], op=ALU.mult),
                       inc="VE")
                setfree(xt_, "VE", ve)
                vv = vfence()
                st_ = dma("sync", "STS",
                          xln_dram[oc * 128:(oc + 1) * 128, tsl], xl,
                          waits=[("VE", vv)])
                E("vector", lambda e, xl=xl, oc=oc, tsl=tsl:
                  e.tensor_copy(xln16[oc][:, tsl], xl), inc="VE")
                pg = P[2] if ts == 0 else P[4]
                last_g = E("tensor",
                           lambda e, xl=xl, oc=oc, pg=pg:
                           e.matmul(pg[0:8, :],
                                    wg_s[:, oc * 8:(oc + 1) * 8],
                                    xl,
                                    start=(oc == 0), stop=(oc == 15)),
                           waits=([("VE", vv)]
                                  + (pw(pg) if oc == 0 else [])),
                           inc="MM")
                setfree(xl, "MM", last_g)
                addfree(xl, "STS", st_)
            setfree(pnm, "VE", p.cnt["VE"])
            setfree(pinv, "VE", p.cnt["VE"])
        sts_xln = p.cnt["STS"]
        mm_gate = p.cnt["MM"]
        for ts in range(2):
            pg = P[2] if ts == 0 else P[4]
            ve = E("vector", lambda e, ts=ts, pg=pg:
                   e.tensor_scalar(gt_sb[:, ts * 512:(ts + 1) * 512],
                                   pg[0:8, :], bg_s[0:8, 0:1], None,
                                   op0=ALU.add),
                   waits=[("MM", mm_gate), ("LDS", ld_consts)], inc="VE")
            setfree(pg, "VE", ve)
        ve_gt = vfence()
        for tt8 in range(8):
            pt2g = P[2 + (tt8 % 2)]
            mmg = E("tensor", lambda e, tt8=tt8, pt2g=pt2g:
                    e.transpose(pt2g[:, 0:8],
                                gt_sb[:, tt8 * 128:(tt8 + 1) * 128],
                                idf_s[0:8, 0:8]),
                    waits=[("VE", ve_gt), ("LDS", ld_consts)] + pw(pt2g),
                    inc="MM")
            ve = E("vector", lambda e, tt8=tt8, pt2g=pt2g:
                   e.tensor_copy(gate_s[:, tt8 * 8:(tt8 + 1) * 8],
                                 pt2g[:, 0:8]),
                   waits=[("MM", mmg)], inc="VE")
            setfree(pt2g, "VE", ve)

        GB = 64

        def gc(t, a, b):
            return scr[:, t * GB + a:t * GB + b]

        for t in range(8):
            E("vector", lambda e, t=t:
              e.tensor_scalar(gc(t, 0, 8), gate_s[:, t * 8:(t + 1) * 8],
                              -1.0, None, op0=ALU.mult))
        for t in range(8):
            E("vector", lambda e, t=t:
              e.reduce_max(gc(t, 48, 49), gc(t, 0, 8), axis=AX.X))
        for t in range(8):
            E("vector", lambda e, t=t:
              e.tensor_scalar(gc(t, 8, 16), gc(t, 0, 8), gc(t, 48, 49),
                              0.0, op0=ALU.subtract, op1=ALU.is_equal))
        for t in range(8):
            E("vector", lambda e, t=t:
              e.scalar_tensor_tensor(gc(t, 16, 24), gc(t, 8, 16), -1.0e30,
                                     gc(t, 0, 8), op0=ALU.mult, op1=ALU.add))
        for t in range(8):
            E("vector", lambda e, t=t:
              e.reduce_max(gc(t, 49, 50), gc(t, 16, 24), axis=AX.X))
        for t in range(8):
            E("vector", lambda e, t=t:
              e.tensor_scalar(gc(t, 24, 32), gc(t, 16, 24), gc(t, 49, 50),
                              0.0, op0=ALU.subtract, op1=ALU.is_equal))
        for t in range(8):
            E("vector", lambda e, t=t:
              e.tensor_tensor(gc(t, 32, 40), gc(t, 8, 16), gc(t, 24, 32),
                              op=ALU.add))
        for t in range(8):
            E("vector", lambda e, t=t:
              e.tensor_scalar(gc(t, 32, 40), gc(t, 32, 40), -1.0, 1.0,
                              op0=ALU.mult, op1=ALU.add))
        ve_nm = None
        for t in range(8):
            ve_nm = E("vector", lambda e, t=t:
                      e.reduce_max(gc(t, 50, 51),
                                   gate_s[:, t * 8:(t + 1) * 8],
                                   axis=AX.X, negate=True), inc="VE")
        sc_exp = None
        for t in range(8):
            sc_exp = E("scalar", lambda e, t=t:
                       e.activation(gc(t, 40, 48),
                                    gate_s[:, t * 8:(t + 1) * 8], AF.Exp,
                                    bias=gc(t, 50, 51), scale=1.0),
                       waits=[("VE", ve_nm)], inc="SC")
        for t in range(8):
            E("vector", lambda e, t=t:
              e.tensor_tensor(gc(t, 40, 48), gc(t, 40, 48), gc(t, 32, 40),
                              op=ALU.mult), waits=[("SC", sc_exp)])
        for t in range(8):
            E("vector", lambda e, t=t:
              e.reduce_sum(gc(t, 51, 52), gc(t, 40, 48), axis=AX.X))
        for t in range(8):
            E("vector", lambda e, t=t:
              e.reciprocal(gc(t, 52, 53), gc(t, 51, 52)))
        for t in range(8):
            E("vector", lambda e, t=t:
              e.tensor_scalar(w_s[:, t * 8:(t + 1) * 8], gc(t, 40, 48),
                              gc(t, 52, 53), None, op0=ALU.mult), inc="VE")

        for tt8 in range(8):
            cs = slice(tt8 * 8, tt8 * 8 + 8)
            E("vector", lambda e, cs=cs:
              e.tensor_copy(w128[:, 0:8], w_s[:, cs]),
              waits=pw(w128))
            ve2 = vfence()
            pt2 = P[2]
            mm = E("tensor", lambda e, pt2=pt2:
                   e.transpose(pt2[:, 0:128], w128[:, :], idf_s[:, :]),
                   waits=[("VE", ve2), ("LDS", ld_consts)] + pw(pt2),
                   inc="MM")
            setfree(w128, "MM", mm)
            ve3 = E("vector", lambda e, tt8=tt8, pt2=pt2:
                    e.tensor_copy(wT6[0:6, tt8 * 128:(tt8 + 1) * 128],
                                  pt2[0:6, 0:128]),
                    waits=[("MM", mm)], inc="VE")
            setfree(pt2, "VE", ve3)
        vsep()
        E("vector", lambda e: e.tensor_copy(wT6b[:, :], wT6[:, :]), inc="VE")
        ve_wT6 = vfence()
        setfree(scr, "SC", p.cnt["SC"])

        def hslot(slot, tc, hc):
            return hsf[slot * 16 + tc * 8 + hc]

        def sush(sc_i, tc):
            return hsf[sc_i * 2 + tc]

        state = {"mm_wave": None, "gbuf": 0}

        def gu_expert(n, slot):
            for tc in range(2):
                pb = P[4]
                mm = E("tensor", lambda e, pb=pb, n=n, tc=tc:
                       e.matmul(pb[:, :],
                                selh_s[0:6, n * 128:(n + 1) * 128],
                                wT6[0:6, tc * 512:(tc + 1) * 512],
                                start=True, stop=True),
                       waits=[("VE", ve_wT6), ("LDS", ld_consts)] + pw(pb),
                       inc="MM")
                ve = E("vector", lambda e, pb=pb, tc=tc:
                       e.tensor_copy(wBt[tc], pb[:, :]),
                       waits=[("MM", mm)] + pw(wBt[tc]), inc="VE")
                setfree(pb, "VE", ve)
            ve_wb = p.cnt["VE"]
            for q4 in range(4):
                hcs = (q4 * 2, q4 * 2 + 1)
                for which in ("g", "u"):
                    wt = Wg_img if which == "g" else Wu_img
                    gb = gslab[state["gbuf"] % 2]
                    state["gbuf"] += 1
                    lds = dma("gpsimd", "LDW", gb, wt[n, q4],
                              waits=pw(gb) + [("VE", ve_wT6)], echo=True)
                    for tc in range(2):
                        for hi, hc in enumerate(hcs):
                            pt = P[4 + ((tc * 2 + hi) % 4)]
                            mm = None
                            for kc in range(KC):
                                mm = E("tensor",
                                       lambda e, pt=pt, kc=kc, hi=hi, tc=tc,
                                       gb=gb:
                                       e.matmul(
                                           pt[:, :],
                                           gb[:, kc * 256 + hi * 128:
                                              kc * 256 + (hi + 1) * 128],
                                           xln16[kc][:,
                                                     tc * 512:(tc + 1) * 512],
                                           start=(kc == 0), stop=(kc == 15)),
                                       waits=([("LDW", lds),
                                               ("VE", ve_wT6)]
                                              + (pw(pt) if kc == 0 else [])),
                                       inc="MM" if kc == 15 else None)
                            if which == "g":
                                sgt = sg[tc * 2 + hi]
                                sc_ = E("scalar",
                                        lambda e, pt=pt, sgt=sgt, hc=hc, n=n:
                                        e.activation(
                                            sgt, pt[:, :], AF.Silu,
                                            bias=bgate_s[:, n * 8 + hc:
                                                         n * 8 + hc + 1],
                                            scale=1.0),
                                        waits=[("MM", mm)] + pw(sgt),
                                        inc="SC")
                                setfree(pt, "SC", sc_)
                                if tc == 1 and hi == 1:
                                    sfence()
                            else:
                                u_ = ut[hi]
                                sgt = sg[tc * 2 + hi]
                                ve = E("vector",
                                       lambda e, pt=pt, hc=hc, n=n, tc=tc,
                                       u_=u_:
                                       e.scalar_tensor_tensor(
                                           u_, pt[:, :],
                                           bup_s[:, n * 8 + hc:
                                                 n * 8 + hc + 1],
                                           wBt[tc],
                                           op0=ALU.add, op1=ALU.mult),
                                       waits=[("MM", mm), ("VE", ve_wb),
                                              ("SC", p.cnt["SC"])]
                                             + pw(u_), inc="VE")
                                setfree(pt, "VE", ve)
                                vsep()
                                hst = hslot(slot, tc, hc)
                                ve = E("vector",
                                       lambda e, hst=hst, sgt=sgt, u_=u_:
                                       e.tensor_tensor(hst, sgt, u_,
                                                       op=ALU.mult),
                                       waits=([("MM", state["mm_wave"])]
                                              if state["mm_wave"] else []),
                                       inc="VE")
                                setfree(u_, "VE", ve)
                                setfree(sgt, "VE", ve)
                    setfree(gb, "MM", p.cnt["MM"])

        def down_wave(wave, ns):
            ve_hs = vfence()
            for dg2 in range(8):
                for i, n in enumerate(ns):
                    rt = rtw[(dg2 * 2 + i) % 2]
                    ldr = dma("gpsimd", "LDW", rt, Wdn_img[dg2, n],
                              waits=pw(rt) + [("VE", ve_wT6)], echo=True)
                    mmlast = None
                    for dc in range(2):
                        for tc in range(2):
                            pdn = P[dc * 2 + tc]
                            slot = n % 2
                            for hc in range(HC):
                                st_mm = (i == 0 and hc == 0)
                                mmlast = E(
                                    "tensor",
                                    lambda e, pdn=pdn, rt=rt, dc=dc, tc=tc,
                                    hc=hc, slot=slot, st_mm=st_mm:
                                    e.matmul(
                                        pdn[:, :],
                                        rt[:, hc * 256 + dc * 128:
                                           hc * 256 + dc * 128 + 128],
                                        hslot(slot, tc, hc),
                                        start=(st_mm and wave == 0) or
                                              (st_mm and wave > 0),
                                        stop=False),
                                    waits=([("LDW", ldr), ("VE", ve_hs)]
                                           + (pw(pdn) if st_mm else [])),
                                    inc="MM" if hc == HC - 1 else None)
                    setfree(rt, "MM", mmlast)
                for dc in range(2):
                    dchunk = dg2 * 2 + dc
                    for tc in range(2):
                        pdn = P[dc * 2 + tc]
                        if wave == 0:
                            mm = E("tensor",
                                   lambda e, pdn=pdn, dchunk=dchunk, tc=tc:
                                   e.matmul(
                                       pdn[:, :],
                                       bdn_s[0:6,
                                             dchunk * 128:(dchunk + 1) * 128],
                                       wT6b[0:6, tc * 512:(tc + 1) * 512],
                                       start=False, stop=True),
                                   waits=[("VE", ve_wT6),
                                          ("LDW", ld_bdn)], inc="MM")
                            ve = E("vector",
                                   lambda e, pdn=pdn, dchunk=dchunk, tc=tc:
                                   e.tensor_copy(
                                       partial[dchunk][:,
                                                       tc * 512:
                                                       (tc + 1) * 512],
                                       pdn[:, :]),
                                   waits=[("MM", mm)], inc="VE")
                        else:
                            ve = E("vector",
                                   lambda e, pdn=pdn, dchunk=dchunk, tc=tc:
                                   e.tensor_tensor(
                                       partial[dchunk][:,
                                                       tc * 512:
                                                       (tc + 1) * 512],
                                       pdn[:, :],
                                       partial[dchunk][:,
                                                       tc * 512:
                                                       (tc + 1) * 512],
                                       op=ALU.add),
                                   waits=[("MM", p.cnt["MM"])], inc="VE")
                        setfree(pdn, "VE", ve)
            state["mm_wave"] = p.cnt["MM"]

        for wv, ns in enumerate(((0, 1), (2, 3), (4, 5))):
            for i, n in enumerate(ns):
                gu_expert(n, i)
            down_wave(wv, ns)

        sc_sup = None
        for q8 in range(8):
            gb = gslab[state["gbuf"] % 2]
            state["gbuf"] += 1
            lds = dma("gpsimd", "LDW", gb, Ws_img[q8],
                      waits=pw(gb) + [("VE", ve_wT6)], echo=True)
            for tc in range(2):
                for si in range(2):
                    sc_i = q8 * 2 + si
                    pt = P[4 + ((tc * 2 + si) % 4)]
                    mm = None
                    for kc in range(KC):
                        mm = E("tensor", lambda e, pt=pt, kc=kc, si=si,
                               tc=tc, gb=gb:
                               e.matmul(pt[:, :],
                                        gb[:, kc * 256 + si * 128:
                                           kc * 256 + (si + 1) * 128],
                                        xln16[kc][:,
                                                  tc * 512:(tc + 1) * 512],
                                        start=(kc == 0), stop=(kc == 15)),
                               waits=([("LDW", lds)]
                                      + (pw(pt) if kc == 0 else [])),
                               inc="MM" if kc == 15 else None)
                    sut = sush(sc_i, tc)
                    sc_sup = E("scalar",
                               lambda e, pt=pt, sut=sut, sc_i=sc_i:
                               e.activation(sut, pt[:, :], AF.Silu,
                                            bias=bsup_s[:, sc_i:sc_i + 1],
                                            scale=1.0),
                               waits=[("MM", mm)] + pw(sut), inc="SC")
                    setfree(pt, "SC", sc_sup)
                    setfree(sut, "SC", sc_sup)
            setfree(gb, "MM", p.cnt["MM"])
        sc_sup = sfence()

        x2_store = {}
        ve_waves = p.cnt["VE"]
        for dg2 in range(8):
            st0 = stw[dg2 % 2]
            lh1 = dma("gpsimd", "LDW", st0, Wsdn_img[dg2],
                      waits=pw(st0) + [("SC", sc_sup), ("VE", ve_waves)],
                      echo=True)
            for dc in range(2):
                dchunk = dg2 * 2 + dc
                for tc in range(2):
                    pdn = P[dc * 2 + tc]
                    mm = None
                    for sc_i in range(KC):
                        mm = E("tensor",
                               lambda e, pdn=pdn, sc_i=sc_i, dc=dc, tc=tc,
                               st0=st0:
                               e.matmul(
                                   pdn[:, :],
                                   st0[:, sc_i * 256 + dc * 128:
                                       sc_i * 256 + dc * 128 + 128],
                                   sush(sc_i, tc),
                                   start=(sc_i == 0), stop=(sc_i == 15)),
                               waits=([("LDW", lh1), ("SC", sc_sup)]
                                      + (pw(pdn) if sc_i == 0 else [])),
                               inc="MM" if sc_i == 15 else None)
                    tsl = slice(tc * 512, (tc + 1) * 512)
                    xl_ = xlt[dchunk % 2]
                    lx = dma("scalar", "LDA", xl_,
                             xln_dram[dchunk * 128:(dchunk + 1) * 128, tsl],
                             waits=pw(xl_) + [("STS", sts_xln)], echo=True)
                    x2_ = x2t[dchunk % 2]
                    E("vector", lambda e, pdn=pdn, dchunk=dchunk, tc=tc,
                      x2_=x2_:
                      e.scalar_tensor_tensor(
                          x2_, pdn[:, :], bsdn_s[:, dchunk:dchunk + 1],
                          partial[dchunk][:, tc * 512:(tc + 1) * 512],
                          op0=ALU.add, op1=ALU.add),
                      waits=[("MM", mm)] + pw(x2_))
                    vsep()
                    ve = E("vector", lambda e, x2_=x2_, xl_=xl_:
                           e.tensor_tensor(x2_, x2_, xl_, op=ALU.add),
                           waits=[("LDA", lx)], inc="VE")
                    setfree(pdn, "VE", ve)
                    setfree(xl_, "VE", ve)
                    vv = vfence()
                    st_ = dma("sync", "STS",
                              x2_dram[dchunk * 128:(dchunk + 1) * 128, tsl],
                              x2_, waits=[("VE", vv)])
                    x2_store[(dchunk, tc)] = st_
                    sq = sqf[dchunk % 2]
                    sc_ = E("scalar", lambda e, x2_=x2_, sq=sq:
                            e.activation(sq, x2_, AF.Square),
                            waits=[("VE", vv)] + pw(sq), inc="SC")
                    sc2 = sfence()
                    pssum = P[4 + tc]
                    pssq = P[6 + tc]
                    first = (dg2 == 0 and dc == 0)
                    mm1 = E("tensor", lambda e, x2_=x2_, pssum=pssum,
                            first=first:
                            e.matmul(pssum[0:1, :], onesc_f[:, 0:1],
                                     x2_, start=first, stop=False),
                            waits=([("VE", vv)]
                                   + (pw(pssum) if first else [])),
                            inc="MM")
                    setfree(x2_, "MM", mm1)
                    addfree(x2_, "SC", sc_)
                    addfree(x2_, "STS", st_)
                    mm2 = E("tensor", lambda e, sq=sq, pssq=pssq,
                            first=first:
                            e.matmul(pssq[0:1, :], onesc_f[:, 0:1],
                                     sq, start=first, stop=False),
                            waits=([("SC", sc2)]
                                   + (pw(pssq) if first else [])),
                            inc="MM")
                    setfree(sq, "MM", mm2)
            setfree(st0, "MM", p.cnt["MM"])
        mm_ln2 = p.cnt["MM"]
        for tc in range(2):
            ve = E("vector", lambda e, tc=tc:
                   e.tensor_copy(nm_r[:, tc * 512:(tc + 1) * 512],
                                 P[4 + tc][0:1, :]),
                   waits=[("MM", mm_ln2)], inc="VE")
            setfree(P[4 + tc], "VE", ve)
            ve = E("vector", lambda e, tc=tc:
                   e.tensor_copy(inv_r[:, tc * 512:(tc + 1) * 512],
                                 P[6 + tc][0:1, :]), inc="VE")
            setfree(P[6 + tc], "VE", ve)
        ve_st2 = ln_stats()
        sts_out = None
        for ts in range(2):
            tsl = slice(ts * 512, (ts + 1) * 512)
            pnm, pinv = P[5], P[3]
            E("tensor", lambda e, pnm=pnm, ts=ts:
              e.matmul(pnm[:, :], onesr_f[0:1, :],
                       nm_r[:, ts * 512:(ts + 1) * 512],
                       start=True, stop=True),
              waits=[("VE", ve_st2)] + pw(pnm), inc="MM")
            mmb = E("tensor", lambda e, pinv=pinv, ts=ts:
                    e.matmul(pinv[:, :], onesr_f[0:1, :],
                             inv_r[:, ts * 512:(ts + 1) * 512],
                             start=True, stop=True),
                    waits=pw(pinv), inc="MM")
            for oc in range(KC):
                x2_ = x2t[oc % 2]
                lx = dma("scalar", "LDA", x2_,
                         x2_dram[oc * 128:(oc + 1) * 128, tsl],
                         waits=[("STS", x2_store[(oc, ts)])] + pw(x2_),
                         echo=True)
                x3_ = x3t[oc % 2]
                E("vector", lambda e, x2_=x2_, x3_=x3_, pnm=pnm:
                  e.tensor_tensor(x3_, x2_, pnm[:, :], op=ALU.add),
                  waits=[("LDA", lx), ("MM", mmb)] + pw(x3_))
                vsep()
                ve = E("vector", lambda e, x3_=x3_, pinv=pinv:
                       e.tensor_tensor(x3_, x3_, pinv[:, :], op=ALU.mult),
                       inc="VE")
                setfree(x2_, "VE", ve)
                vv = vfence()
                sts_out = dma("sync", "STS",
                              out[oc * 128:(oc + 1) * 128, tsl],
                              x3_, waits=[("VE", vv)])
                setfree(x3_, "STS", sts_out)
            setfree(pnm, "VE", p.cnt["VE"])
            setfree(pinv, "VE", p.cnt["VE"])
        E("sync", lambda e: None, waits=[("STS", sts_out)])

        def runner(name):
            def run(eng):
                for need, fn, inc, n in p.steps[name]:
                    for s, v in need:
                        eng.wait_ge(sem[s], v)
                    if fn is None:
                        continue
                    inst = fn(eng)
                    if inc is not None:
                        inst.then_inc(sem[inc], n)
            return run

        block.sync(runner("sync"))
        block.gpsimd(runner("gpsimd"))
        block.tensor(runner("tensor"))
        block.vector(runner("vector"))
        block.scalar(runner("scalar"))

    return nc


_NC_CACHE = {}


def _build_cached():
    if "nc" not in _NC_CACHE:
        _NC_CACHE["nc"] = build_nc()
    return _NC_CACHE["nc"]


def _rope_tables():
    inv = 1.0 / (10000.0 ** (np.arange(0, HD, 2, dtype=np.float64) / HD))
    t = np.arange(S, dtype=np.float64)
    fr = t[:, None] * inv[None, :]
    emb = np.concatenate([fr, fr], axis=1)
    cos = np.cos(emb).T.astype(np.float32)
    sin = np.sin(emb).T.astype(np.float32)
    return cos, sin


def _cols(vec, n):
    return np.ascontiguousarray(
        np.asarray(vec).reshape(n, 128).T.astype(np.float32))


def kernel(**inputs):
    f = {k: np.asarray(v) for k, v in inputs.items()}
    x = f["L0_attn_hidden_states"].astype(np.float32, copy=False)
    wq, bq = f["L0_attn_wq"], f["L0_attn_bq"]
    wk, bk = f["L0_attn_wk"], f["L0_attn_bk"]
    wv, bv = f["L0_attn_wv"], f["L0_attn_bv"]
    wo, bo = f["L0_attn_wo"], f["L0_attn_bo"]

    cos, sin = _rope_tables()
    cosq = (cos * SCALE).astype(BF)
    sinq = (sin * SCALE).astype(BF)
    cosk = cos.astype(BF)
    sink = sin.astype(BF)
    rsT = np.zeros((128, 128), dtype=np.float32)
    for i_ in range(64):
        rsT[64 + i_, i_] = -1.0
        rsT[i_, 64 + i_] = 1.0
    rsT = rsT.astype(BF)
    dm = np.zeros((128, 4 * 512), dtype=np.float32)
    ii = np.arange(128)[:, None]
    jj = np.arange(512)[None, :]
    for m in range(4):
        dm[:, m * 512:(m + 1) * 512] = (m * 128 + ii <= jj)
    dmask = dm.astype(BF)
    idf = np.eye(128, dtype=np.float32)
    selh = np.zeros((NR, NR * 128), dtype=np.float32)
    for n in range(NR):
        selh[n, n * 128:(n + 1) * 128] = 1.0

    woT = np.ascontiguousarray(wo.T)

    def img16(w2d, qw):
        m = w2d.shape[1]
        return np.ascontiguousarray(
            w2d.reshape(KC, 128, m // qw, qw).transpose(2, 1, 0, 3).reshape(
                m // qw, 128, KC * qw).astype(np.float32))

    Wdn_img = np.ascontiguousarray(
        f["L0_ffn_Wdown"].reshape(NR, HC, 128, 8, 256).transpose(
            3, 0, 2, 1, 4).reshape(8, NR, 128, 2048).astype(np.float32))
    Wsdn_img = img16(f["L0_ffn_Ws_down"], 256)
    Wg2 = np.ascontiguousarray(
        f["L0_ffn_Wg"].reshape(KC, 128, 8).transpose(1, 0, 2).reshape(128, 128))

    common = dict(
        Wg2=Wg2, bgr=f["L0_ffn_bg"].reshape(8, 1).astype(np.float32),
        Wg_img=np.stack([img16(f["L0_ffn_Wgate"][n], 256)
                         for n in range(NR)]),
        Wu_img=np.stack([img16(f["L0_ffn_Wup"][n], 256)
                         for n in range(NR)]),
        bgatec=_cols(f["L0_ffn_bgate"].reshape(-1), NR * HC),
        bupc=_cols(f["L0_ffn_bup"].reshape(-1), NR * HC),
        Wdn_img=Wdn_img, bdn=f["L0_ffn_bdown"].astype(np.float32),
        Ws_img=img16(f["L0_ffn_Ws_up"].astype(np.float32), 256),
        bsupc=_cols(f["L0_ffn_bs_up"], KC),
        Wsdn_img=Wsdn_img, bsdnc=_cols(f["L0_ffn_bs_down"], KC),
        boc=_cols(bo, KC),
        cosq=cosq, sinq=sinq, cosk=cosk, sink=sink, dmask=dmask, idf=idf,
        selh=selh, rsT=rsT,
    )

    in_maps = []
    for c in range(8):
        b, par = c // 2, c % 2
        hsl = slice(par * 1024, (par + 1) * 1024)
        psl = slice((1 - par) * 1024, (2 - par) * 1024)
        sel = np.zeros((128, 4), dtype=np.float32)
        if par == 0:
            sel[:, 1] = 1.0
            sel[:, 2] = 1.0
        else:
            sel[:, 0] = 1.0
            sel[:, 3] = 1.0
        m = dict(common)
        m.update(
            xT=np.ascontiguousarray(x[b].T),
            xownT=np.ascontiguousarray(x[b, hsl].T),
            wq_img=img16(np.ascontiguousarray(wq[hsl].T.astype(np.float32)), 256),
            wk_img=img16(np.ascontiguousarray(wk[hsl].T.astype(np.float32)), 256),
            wv_img=img16(np.ascontiguousarray(wv[hsl].T.astype(np.float32)), 256),
            woTl=np.ascontiguousarray(
                np.concatenate([woT[hsl], woT[psl]], axis=0).astype(
                    np.float32)),
            bqc=_cols(bq[hsl], NHL), bkc=_cols(bk[hsl], NHL),
            bvr=np.asarray(bv[hsl]).reshape(1, -1).astype(np.float32),
            selc=sel,
        )
        in_maps.append(m)

    nc = _build_cached()
    res = run_bass_kernel_spmd(nc, in_maps, core_ids=list(range(8)))

    y = np.empty((B, S, DM), dtype=np.float32)
    for c in range(8):
        b, par = c // 2, c % 2
        y[b, par * 1024:(par + 1) * 1024, :] = res.results[c]["out"].T
    return y


# version 13
# speedup vs baseline: 1.0313x; 1.0313x over previous
import numpy as np
import ml_dtypes
from contextlib import ExitStack

import concourse.bass as bass
import concourse.mybir as mybir
from concourse.bass_utils import run_bass_kernel_spmd

F32 = mybir.dt.float32
BF16 = mybir.dt.bfloat16
AF = mybir.ActivationFunctionType
ALU = mybir.AluOpType
AX = mybir.AxisListType
BF = ml_dtypes.bfloat16

B, S, DM = 4, 2048, 2048
NHL, HD, GRP, KC = 8, 128, 2, 16
TOK = 1024
NR, H, HC = 6, 1024, 8
SCALE = HD ** -0.5
EPS = 1e-5

ENGINES = ("sync", "gpsimd", "tensor", "vector", "scalar")
SEMS = ("LDW", "LDS", "STS", "MM", "VE", "SC", "CC", "LDA")


class Prog:
    def __init__(self):
        self.steps = {e: [] for e in ENGINES}
        self.cnt = {s: 0 for s in SEMS}
        self.ensured = {e: {s: 0 for s in SEMS} for e in ENGINES}

    def emit(self, eng, fn, waits=(), inc=None, n=1):
        need = []
        for s, v in waits:
            if v is None:
                continue
            assert v <= self.cnt[s], f"wait {s}>={v} > emitted {self.cnt[s]}"
            if v > self.ensured[eng][s]:
                need.append((s, v))
                self.ensured[eng][s] = v
        if inc is not None:
            self.cnt[inc] += n
        self.steps[eng].append((tuple(need), fn, inc, n))
        return self.cnt[inc] if inc is not None else None


class View:

    def __init__(self, arena, off, n, w):
        self.a, self.off, self.n, self.w = arena, off, n, w

    def __getitem__(self, i):
        assert 0 <= i < self.n
        if not hasattr(self, "_c"):
            self._c = {}
        if i not in self._c:
            o = self.off + i * self.w
            self._c[i] = self.a[:, o:o + self.w]
        return self._c[i]


def build_nc():
    nc = bass.Bass("TRN2", target_bir_lowering=False)
    d = {}

    def inp(name, shape, dt=F32):
        d[name] = nc.dram_tensor(name, list(shape), dt, kind="ExternalInput").ap()
        return d[name]

    xT = inp("xT", (DM, S))
    xownT = inp("xownT", (DM, TOK))
    wq_img = inp("wq_img", (4, 128, 4096))
    wk_img = inp("wk_img", (4, 128, 4096))
    wv_img = inp("wv_img", (4, 128, 4096))
    woTl = inp("woTl", (DM, DM))
    bqc = inp("bqc", (128, NHL))
    bkc = inp("bkc", (128, NHL))
    bvr = inp("bvr", (1, NHL * HD))
    boc = inp("boc", (128, KC))
    Wg2 = inp("Wg2", (128, 128))
    bgr = inp("bgr", (8, 1))
    Wg_img = inp("Wg_img", (NR, 4, 128, 4096))
    Wu_img = inp("Wu_img", (NR, 4, 128, 4096))
    bgatec = inp("bgatec", (128, NR * HC))
    bupc = inp("bupc", (128, NR * HC))
    Wdn_img = inp("Wdn_img", (8, NR, 128, 2048))
    bdn = inp("bdn", (NR, DM))
    Ws_img = inp("Ws_img", (8, 128, 4096))
    bsupc = inp("bsupc", (128, KC))
    Wsdn_img = inp("Wsdn_img", (8, 128, 4096))
    bsdnc = inp("bsdnc", (128, KC))
    cosq_d = inp("cosq", (128, S), BF16)
    sinq_d = inp("sinq", (128, S), BF16)
    cosk_d = inp("cosk", (128, S), BF16)
    sink_d = inp("sink", (128, S), BF16)
    dmask = inp("dmask", (128, 4 * 512), BF16)
    idf = inp("idf", (128, 128))
    selc = inp("selc", (128, 4))
    selh = inp("selh", (NR, NR * 128))
    rsT_d = inp("rsT", (128, 128), BF16)

    out = nc.dram_tensor("out", [DM, TOK], F32, kind="ExternalOutput").ap()

    cc_in_a = nc.dram_tensor("cc_in_a", [4 * HD, TOK], BF16).ap()
    cc_out_a = nc.dram_tensor("cc_out_a", [8 * HD, TOK], BF16).ap()
    cc_in_b = nc.dram_tensor("cc_in_b", [4 * HD, TOK], BF16).ap()
    cc_out_b = nc.dram_tensor("cc_out_b", [8 * HD, TOK], BF16).ap()
    x1_dram = nc.dram_tensor("x1_dram", [DM, TOK], F32).ap()
    xln_dram = nc.dram_tensor("xln_dram", [DM, TOK], F32).ap()
    x2_dram = nc.dram_tensor("x2_dram", [DM, TOK], F32).ap()
    sep_dram = nc.dram_tensor("sep_dram", [1, 64], F32).ap()

    p = Prog()
    E = p.emit
    es = ExitStack()

    def sbt(name, shape, dt):
        return es.enter_context(nc.sbuf_tensor(name, list(shape), dt))

    with es:
        AXa = sbt("AXa", (128, 32768), BF16)
        AYa = sbt("AYa", (128, 16384), BF16)
        AWa = sbt("AWa", (128, 11264), BF16)
        AVa = sbt("AVa", (128, 4096), BF16)
        AFa = sbt("AFa", (128, 7168), F32)

        xs = View(AXa, 0, KC, S)
        wo_slab = View(AXa, 0, KC, DM)
        xln16 = View(AXa, 0, KC, TOK)
        partial = View(AXa, KC * TOK, KC, TOK)
        qrot = View(AYa, 0, GRP, S)
        krot = View(AYa, GRP * S, GRP, S)
        vtm = View(AYa, 2 * GRP * S, KC, 256)
        wslab = View(AYa, 2 * GRP * S + KC * 256, KC, 256)
        myown = View(AYa, 0, NHL, TOK)
        partner = View(AYa, NHL * TOK, NHL, TOK)
        hsf = View(AYa, 0, 32, 512)
        c_cosq = View(AWa, 0, 1, S)[0]
        c_sinq = View(AWa, S, 1, S)[0]
        c_cosk = View(AWa, 2 * S, 1, S)[0]
        c_sink = View(AWa, 3 * S, 1, S)[0]
        maskt = View(AWa, 4 * S, 4, 512)
        rcv = View(AWa, 0, 4, TOK)
        sendt = View(AWa, 4 * TOK, 2, TOK)
        gslab = View(AWa, 0, 2, 4096)
        sg = View(AWa, 8192, 4, 512)
        ut = View(AWa, 8192 + 4 * 512, 2, 512)
        stw = View(AWa, 0, 2, 4096)
        probs = View(AVa, 0, 3, 512)
        sq512 = View(AVa, 3 * 512, 1, 512)[0]
        rope16 = View(AVa, 2048, 2, 512)
        sendE = View(AVa, 3072, 1, 1024)[0]
        rtw = View(AVa, 0, 2, HC * 256)
        ropet = View(AFa, 0, 3, 512)
        rB = View(AFa, 3 * 512, 1, 512)[0]
        ssum_t = AFa[0:1, 2048:2560]
        xot = View(AFa, 0, 2, 512)
        x1t = View(AFa, 2 * 512, 2, 512)
        sqf = View(AFa, 4 * 512, 2, 512)
        xlnt = View(AFa, 6 * 512, 2, 512)
        wBt = View(AFa, 0, 2, 512)
        x2t = View(AFa, 2 * 512, 2, 512)
        x3t = View(AFa, 6 * 512, 2, 512)
        xlt = View(AFa, 6 * 512, 2, 512)
        nm_r = AFa[0:1, 4096:5120]
        inv_r = AFa[0:1, 5120:6144]
        tmp_r = AFa[0:1, 6144:7168]
        gt_sb = AFa[0:8, 6144:7168]

        attnfm = sbt("attnfm", (128, NHL * S), BF16)
        atv = View(attnfm, 0, NHL, S)
        idf_s = sbt("idf_s", (128, 128), F32)
        selc_s = sbt("selc_s", (128, 4), F32)
        bq_s = sbt("bq_s", (128, NHL), F32)
        bk_s = sbt("bk_s", (128, NHL), F32)
        bv_s = sbt("bv_s", (1, NHL * HD), BF16)
        bo_s = sbt("bo_s", (128, KC), F32)
        wg_s = sbt("wg_s", (128, 128), F32)
        bg_s = sbt("bg_s", (8, 1), F32)
        bgate_s = sbt("bgate_s", (128, NR * HC), F32)
        bup_s = sbt("bup_s", (128, NR * HC), F32)
        bdn_s = sbt("bdn_s", (NR, DM), BF16)
        wT6b = sbt("wT6b", (NR, TOK), BF16)
        bsup_s = sbt("bsup_s", (128, KC), F32)
        bsdn_s = sbt("bsdn_s", (128, KC), F32)
        ones_b = sbt("ones_b", (128, 1), BF16)
        onesr_b = sbt("onesr_b", (1, 128), BF16)
        onesr_f = sbt("onesr_f", (1, 128), F32)
        onesc_f = sbt("onesc_f", (128, 1), F32)
        scr = sbt("scr", (128, 512), F32)
        gate_s = sbt("gate_s", (128, 64), F32)
        w_s = sbt("w_s", (128, 64), F32)
        w128 = sbt("w128", (128, 128), F32)
        wT6 = sbt("wT6", (NR, TOK), F32)
        negb = sbt("negb", (128, 1), F32)
        selh_s = sbt("selh_s", (NR, NR * 128), F32)
        rsT_s = sbt("rsT_s", (128, 128), BF16)
        eps_s = sbt("eps_s", (1, 1), F32)
        sep16 = sbt("sep16", (128, 16), BF16)
        sepd_s = sbt("sepd_s", (1, 64), F32)

        P = [es.enter_context(nc.psum_tensor(f"P{i}", [128, 512], F32))
             for i in range(8)]
        sem = {s: es.enter_context(nc.semaphore(s)) for s in SEMS}
        block = es.enter_context(nc.Block())

        pfree = {}

        def key(ap_or_obj):
            return id(ap_or_obj)

        def pw(*objs):
            w = []
            for t in objs:
                w.extend(pfree.get(key(t), []))
            return w

        def setfree(obj, semname, val):
            pfree[key(obj)] = [(semname, val)]

        def addfree(obj, semname, val):
            pfree.setdefault(key(obj), []).append((semname, val))

        def vsep():
            E("vector", lambda e:
              e.tensor_copy(sep16[:, 0:8], sep16[:, 8:16]))

        def vfence():
            vsep()
            vsep()
            return E("vector", lambda e:
                     e.tensor_copy(sep16[:, 0:8], sep16[:, 8:16]), inc="VE")

        def sfence():
            E("scalar", lambda e:
              e.activation(sep16[:, 0:8], sep16[:, 8:16], AF.Copy))
            return E("scalar", lambda e:
                     e.activation(sep16[:, 0:8], sep16[:, 8:16], AF.Copy),
                     inc="SC")

        def dma(eng, sm, outap, inap, waits=(), echo=False):
            r = E(eng, lambda e, o=outap, i=inap: e.dma_start(out=o, in_=i),
                  waits=waits, inc=sm, n=16)
            if echo:
                r = E(eng, lambda e: e.dma_start(out=sepd_s[:, :],
                                                 in_=sep_dram[:, :]),
                      inc=sm, n=16)
            return r

        for t_, src in ((c_cosq, cosq_d), (c_sinq, sinq_d), (c_cosk, cosk_d),
                        (c_sink, sink_d), (idf_s[:, :], idf),
                        (selc_s[:, :], selc), (bq_s[:, :], bqc),
                        (bk_s[:, :], bkc), (bo_s[:, :], boc),
                        (wg_s[:, :], Wg2), (bg_s[:, :], bgr),
                        (bgate_s[:, :], bgatec), (bup_s[:, :], bupc),
                        (bsup_s[:, :], bsupc),
                        (selh_s[:, :], selh), (rsT_s[:, :], rsT_d),
                        (bsdn_s[:, :], bsdnc)):
            dma("sync", "LDS", t_, src[:, :])
        for m in range(4):
            dma("sync", "LDS", maskt[m], dmask[:, m * 512:(m + 1) * 512])
        ld_consts = dma("sync", "LDS", sepd_s[:, :], sep_dram[:, :],
                        echo=True)
        ld_bv = dma("gpsimd", "LDW", bv_s[:, :], bvr[:, :], echo=True)
        ld_bdn = dma("gpsimd", "LDW", bdn_s[:, :], bdn[:, :], echo=True)

        ms_final = None
        for ap_, v in ((ones_b[:, :], 1.0), (onesr_b[:, :], 1.0),
                       (onesr_f[:, :], 1.0), (onesc_f[:, :], 1.0),
                       (eps_s[:, :], float(EPS)),
                       (negb[:, :], -20.0),
                       (w128[:, :], 0.0)):
            ms_final = E("vector", lambda e, a=ap_, vv=v: e.memset(a, vv),
                         inc="VE")
        ms_final = vfence()

        ldx = {}
        for kc in range(KC):
            ldx[kc] = dma("gpsimd", "LDW", xs[kc],
                          xT[kc * 128:(kc + 1) * 128, :], echo=True)

        for g in range(4):
            cols = slice(g * 256, (g + 1) * 256)
            for wsrc in ("q", "k", "v"):
                wd = {"q": wq_img, "k": wk_img, "v": wv_img}[wsrc]
                ldslab = dma("gpsimd", "LDW",
                             AYa[:, wslab.off:wslab.off + 4096], wd[g],
                             waits=pw(wslab), echo=True)
                if wsrc in ("q", "k"):
                    bias = bq_s if wsrc == "q" else bk_s
                    cos = c_cosq if wsrc == "q" else c_cosk
                    sinm = c_sinq if wsrc == "q" else c_sink
                    dst = qrot if wsrc == "q" else krot
                    for hl in range(GRP):
                        h = g * GRP + hl
                        for sl in range(4):
                            pt = P[(0, 1, 5, 6)[sl % 4]]
                            mm = None
                            for kc in range(KC):
                                mm = E("tensor",
                                       lambda e, pt=pt, kc=kc, hl=hl, sl=sl:
                                       e.matmul(
                                           pt[:, :],
                                           wslab[kc][:,
                                                     hl * 128:(hl + 1) * 128],
                                           xs[kc][:, sl * 512:(sl + 1) * 512],
                                           start=(kc == 0), stop=(kc == 15)),
                                       waits=([("LDW", ldslab),
                                               ("LDW", ldx[kc])]
                                              + (pw(pt) if kc == 0 else [])),
                                       inc="MM" if kc == 15 else None)
                            dsl = slice(sl * 512, (sl + 1) * 512)
                            qb = rope16[sl % 2]
                            ve = E("vector",
                                   lambda e, pt=pt, h=h, bias=bias, qb=qb:
                                   e.tensor_scalar(qb, pt[:, :],
                                                   bias[:, h:h + 1], None,
                                                   op0=ALU.add),
                                   waits=[("MM", mm), ("LDS", ld_consts)]
                                         + pw(qb), inc="VE")
                            setfree(pt, "VE", ve)
                            prot = P[2 + (sl % 2)]
                            mmr = E("tensor", lambda e, prot=prot, qb=qb:
                                    e.matmul(prot[:, :], rsT_s[:, :], qb,
                                             start=True, stop=True),
                                    waits=[("VE", ve)] + pw(prot), inc="MM")
                            addfree(qb, "MM", mmr)
                            ve2 = E("vector",
                                    lambda e, dsl=dsl, sinm=sinm, prot=prot:
                                    e.tensor_tensor(ropet[1], prot[:, :],
                                                    sinm[:, dsl],
                                                    op=ALU.mult),
                                    waits=[("MM", mmr)], inc="VE")
                            setfree(prot, "VE", ve2)
                            E("vector", lambda e, dsl=dsl, cos=cos, qb=qb:
                              e.tensor_tensor(ropet[0], qb,
                                              cos[:, dsl], op=ALU.mult))
                            vsep()
                            ve = E("vector",
                                   lambda e, hl=hl, dsl=dsl, dst=dst:
                                   e.tensor_tensor(dst[hl][:, dsl], ropet[0],
                                                   ropet[1], op=ALU.add),
                                   inc="VE")
                            addfree(qb, "VE", ve)
                else:
                    for tcv in range(KC):
                        pt = P[(0, 1, 5, 6)[tcv % 4]]
                        for kc in range(KC):
                            E("tensor", lambda e, pt=pt, kc=kc, tcv=tcv:
                              e.matmul(pt[:, 0:256],
                                       xs[kc][:, tcv * 128:(tcv + 1) * 128],
                                       wslab[kc],
                                       start=(kc == 0), stop=False),
                              waits=([("LDW", ldslab), ("LDW", ldx[kc])]
                                     + (pw(pt) if kc == 0 else [])),
                              inc=None)
                        mm = E("tensor", lambda e, pt=pt, g=g:
                               e.matmul(pt[:, 0:256], onesr_b[0:1, :],
                                        bv_s[0:1, g * 256:(g + 1) * 256],
                                        start=False, stop=True),
                               waits=[("VE", ms_final), ("LDW", ld_bv)],
                               inc="MM")
                        ve = E("vector", lambda e, pt=pt, tcv=tcv:
                               e.tensor_copy(vtm[tcv], pt[:, 0:256]),
                               waits=[("MM", mm)], inc="VE")
                        setfree(pt, "VE", ve)
                setfree(wslab, "MM", p.cnt["MM"])
            ve_qkv = vfence()
            if g == 3:
                mm_xs_free = p.cnt["MM"]

            for hl in range(GRP):
                h = g * GRP + hl
                for qs in range(4):
                    kcount = 4 * (qs + 1)
                    ppv = P[(0, 1, 2)[(g * GRP * 4 + hl * 4 + qs) % 3]]
                    psm = P[6]
                    qsl = slice(qs * 512, (qs + 1) * 512)
                    for kb in range(kcount):
                        pt = P[(3, 4, 7)[kb % 3]]
                        mm = E("tensor", lambda e, pt=pt, hl=hl, kb=kb,
                               qsl=qsl:
                               e.matmul(pt[:, :],
                                        krot[hl][:, kb * 128:(kb + 1) * 128],
                                        qrot[hl][:, qsl],
                                        start=True, stop=True),
                               waits=[("VE", ve_qkv)] + pw(pt), inc="MM")
                        pr = probs[kb % 3]
                        sc_ = E("scalar", lambda e, pt=pt, pr=pr:
                                e.activation(pr, pt[:, :], AF.Exp,
                                             bias=negb[:, 0:1],
                                             scale=1.0),
                                waits=[("MM", mm), ("VE", ms_final)] + pw(pr),
                                inc="SC")
                        setfree(pt, "SC", sc_)
                        dep = ("SC", sc_)
                        m = kb - 4 * qs
                        if m >= 0:
                            vv = E("vector", lambda e, pr=pr, m=m:
                                   e.tensor_tensor(pr, pr, maskt[m],
                                                   op=ALU.mult),
                                   waits=[dep, ("LDS", ld_consts)], inc="VE")
                            dep = ("VE", vv)
                        first = (kb == 0)
                        last = (kb == kcount - 1)
                        E("tensor", lambda e, ppv=ppv, kb=kb, hl=hl, pr=pr,
                          first=first, last=last:
                          e.matmul(ppv[:, :],
                                   vtm[kb][:, hl * 128:(hl + 1) * 128],
                                   pr, start=first, stop=last),
                          waits=[dep] + (pw(ppv) if first else []),
                          inc="MM" if last else None)
                        E("tensor", lambda e, psm=psm, pr=pr,
                          first=first, last=last:
                          e.matmul(psm[0:1, :], ones_b[:, 0:1],
                                   pr, start=first, stop=last),
                          waits=(pw(psm) if first else []),
                          inc="MM" if last else None)
                        setfree(pr, "MM", p.cnt["MM"])
                    mm_all = p.cnt["MM"]
                    ve = E("vector", lambda e, psm=psm:
                           e.tensor_copy(ssum_t, psm[0:1, :]),
                           waits=[("MM", mm_all)] + pw(ssum_t), inc="VE")
                    setfree(psm, "VE", ve)
                    ve = E("vector", lambda e:
                           e.reciprocal(ssum_t, ssum_t), inc="VE")
                    pb = P[5]
                    mm = E("tensor", lambda e, pb=pb:
                           e.matmul(pb[:, :], onesr_f[0:1, :],
                                    ssum_t, start=True, stop=True),
                           waits=[("VE", ve)] + pw(pb), inc="MM")
                    setfree(ssum_t, "MM", mm)
                    ve = E("vector", lambda e, pb=pb:
                           e.tensor_copy(rB, pb[:, :]),
                           waits=[("MM", mm)], inc="VE")
                    setfree(pb, "VE", ve)
                    ve = E("vector", lambda e, h=h, qsl=qsl, ppv=ppv:
                           e.tensor_tensor(atv[h][:, qsl], ppv[:, :],
                                           rB, op=ALU.mult), inc="VE")
                    setfree(ppv, "VE", ve)

            if g in (1, 3):
                half = g // 2
                ccin = cc_in_a if half == 0 else cc_in_b
                for cl in range(4):
                    c = half * 4 + cl
                    sd = sendE
                    E("vector", lambda e, c=c, sd=sd:
                      e.tensor_scalar(sd, atv[c][:, 0:TOK],
                                      selc_s[:, 0:1], None, op0=ALU.mult),
                      waits=[("LDS", ld_consts)] + pw(sd))
                    vsep()
                    ve = E("vector", lambda e, c=c, sd=sd:
                           e.scalar_tensor_tensor(sd, atv[c][:, TOK:S],
                                                  selc_s[:, 1:2], sd,
                                                  op0=ALU.mult, op1=ALU.add),
                           inc="VE")
                    st_ = dma("sync", "STS",
                              ccin[cl * 128:(cl + 1) * 128, :], sd,
                              waits=[("VE", ve)])
                    setfree(sd, "STS", st_)
                if half == 0:
                    sts_send_a = p.cnt["STS"]
                else:
                    sts_send_b = p.cnt["STS"]

        mm_endB = p.cnt["MM"]
        ve_attn = p.cnt["VE"]
        sc_endB = p.cnt["SC"]

        for c in range(NHL):
            E("vector", lambda e, c=c:
              e.tensor_scalar(myown[c], atv[c][:, 0:TOK],
                              selc_s[:, 2:3], None, op0=ALU.mult),
              waits=[("VE", ve_attn), ("LDS", ld_consts), ("MM", mm_endB),
                     ("SC", sc_endB)])
        for c in range(NHL):
            E("vector", lambda e, c=c:
              e.scalar_tensor_tensor(myown[c], atv[c][:, TOK:S],
                                     selc_s[:, 3:4], myown[c],
                                     op0=ALU.mult, op1=ALU.add),
              inc="VE")
        for kc in range(KC):
            dma("gpsimd", "LDW", wo_slab[kc],
                woTl[kc * 128:(kc + 1) * 128, :],
                waits=[("MM", mm_xs_free)],
                echo=(kc == KC - 1))
        ld_wo = p.cnt["LDW"]
        E("gpsimd", None, waits=[("LDW", ld_wo)])
        cc = E("gpsimd", lambda e:
               e.collective_compute(
                   "AllGather", ALU.bypass,
                   replica_groups=[[0, 1], [2, 3], [4, 5], [6, 7]],
                   ins=[cc_in_a[:, :]], outs=[cc_out_a[:, :]]),
               waits=[("STS", sts_send_a)], inc="CC")
        cc = E("gpsimd", lambda e:
               e.collective_compute(
                   "AllGather", ALU.bypass,
                   replica_groups=[[0, 1], [2, 3], [4, 5], [6, 7]],
                   ins=[cc_in_b[:, :]], outs=[cc_out_b[:, :]]),
               waits=[("STS", sts_send_b)], inc="CC")
        for c in range(NHL):
            half = c // 4
            cl = c % 4
            ccout = cc_out_a if half == 0 else cc_out_b
            r0 = rcv[(c % 2) * 2]
            r1 = rcv[(c % 2) * 2 + 1]
            l0 = dma("sync", "LDS", r0, ccout[cl * 128:(cl + 1) * 128, :],
                     waits=[("CC", 1 + half), ("VE", ve_attn)] + pw(r0),
                     echo=True)
            l1 = dma("sync", "LDS", r1,
                     ccout[(4 + cl) * 128:(4 + cl + 1) * 128, :],
                     waits=pw(r1), echo=True)
            E("vector", lambda e, c=c, r0=r0:
              e.tensor_scalar(partner[c], r0, selc_s[:, 0:1], None,
                              op0=ALU.mult),
              waits=[("LDS", l0)])
            vsep()
            ve = E("vector", lambda e, c=c, r1=r1:
                   e.scalar_tensor_tensor(partner[c], r1, selc_s[:, 1:2],
                                          partner[c], op0=ALU.mult,
                                          op1=ALU.add),
                   waits=[("LDS", l1)], inc="VE")
            setfree(r0, "VE", ve)
            setfree(r1, "VE", ve)
        ve_part = vfence()

        x1_store = {}
        for ts in range(2):
            tsl = slice(ts * 512, (ts + 1) * 512)
            for oc in range(KC):
                pt = P[(0, 1, 2, 4)[oc % 4]]
                mm = None
                for kc in range(KC):
                    rhs = myown[kc] if kc < 8 else partner[kc - 8]
                    mm = E("tensor", lambda e, pt=pt, kc=kc, oc=oc, rhs=rhs,
                           tsl=tsl:
                           e.matmul(pt[:, :],
                                    wo_slab[kc][:, oc * 128:(oc + 1) * 128],
                                    rhs[:, tsl],
                                    start=(kc == 0), stop=(kc == 15)),
                           waits=([("LDW", ld_wo), ("VE", ve_part)]
                                  + (pw(pt) if kc == 0 else [])),
                           inc="MM" if kc == 15 else None)
                xo = xot[oc % 2]
                lx = dma("scalar", "LDA", xo,
                         xownT[oc * 128:(oc + 1) * 128, tsl],
                         waits=[("MM", mm_endB), ("VE", ve_attn)] + pw(xo),
                         echo=True)
                xt_ = x1t[oc % 2]
                ve = E("vector", lambda e, pt=pt, oc=oc, xo=xo, xt_=xt_:
                       e.scalar_tensor_tensor(xt_, pt[:, :],
                                              bo_s[:, oc:oc + 1], xo,
                                              op0=ALU.add, op1=ALU.add),
                       waits=[("MM", mm), ("LDA", lx)] + pw(xt_), inc="VE")
                setfree(pt, "VE", ve)
                setfree(xo, "VE", ve)
                vv = vfence()
                st_ = dma("sync", "STS",
                          x1_dram[oc * 128:(oc + 1) * 128, tsl], xt_,
                          waits=[("VE", vv)])
                x1_store[(oc, ts)] = st_
                sq = sqf[oc % 2]
                sc_ = E("scalar", lambda e, xt_=xt_, sq=sq:
                        e.activation(sq, xt_, AF.Square),
                        waits=[("VE", vv)] + pw(sq), inc="SC")
                sc2 = sfence()
                mm1 = E("tensor", lambda e, xt_=xt_, oc=oc:
                        e.matmul(P[6][0:1, :], onesc_f[:, 0:1], xt_,
                                 start=(oc == 0), stop=(oc == 15)),
                        waits=([("VE", vv)]
                               + (pw(P[6]) if oc == 0 else [])),
                        inc="MM")
                setfree(xt_, "MM", mm1)
                addfree(xt_, "SC", sc_)
                addfree(xt_, "STS", st_)
                mm2 = E("tensor", lambda e, sq=sq, oc=oc:
                        e.matmul(P[7][0:1, :], onesc_f[:, 0:1], sq,
                                 start=(oc == 0), stop=(oc == 15)),
                        waits=([("SC", sc2)]
                               + (pw(P[7]) if oc == 0 else [])),
                        inc="MM")
                setfree(sq, "MM", mm2)
            ve = E("vector", lambda e, ts=ts:
                   e.tensor_copy(nm_r[:, ts * 512:(ts + 1) * 512],
                                 P[6][0:1, :]),
                   waits=[("MM", p.cnt["MM"])], inc="VE")
            setfree(P[6], "VE", ve)
            ve = E("vector", lambda e, ts=ts:
                   e.tensor_copy(inv_r[:, ts * 512:(ts + 1) * 512],
                                 P[7][0:1, :]), inc="VE")
            setfree(P[7], "VE", ve)

        def ln_stats():
            E("vector", lambda e:
              e.tensor_scalar(nm_r, nm_r, -1.0 / DM, None, op0=ALU.mult))
            E("vector", lambda e:
              e.tensor_scalar(inv_r, inv_r, 1.0 / DM, None, op0=ALU.mult))
            E("vector", lambda e:
              e.tensor_tensor(tmp_r, nm_r, nm_r, op=ALU.mult))
            ve = E("vector", lambda e:
                   e.scalar_tensor_tensor(tmp_r, tmp_r, -1.0, inv_r,
                                          op0=ALU.mult, op1=ALU.add),
                   inc="VE")
            sc_ = E("scalar", lambda e:
                    e.activation(tmp_r, tmp_r, AF.Sqrt,
                                 bias=eps_s[0:1, 0:1], scale=1.0),
                    waits=[("VE", ve)], inc="SC")
            E("vector", lambda e:
              e.reciprocal(inv_r, tmp_r),
              waits=[("SC", sc_)], inc="VE")
            return vfence()

        ve_st1 = ln_stats()
        for ts in range(2):
            tsl = slice(ts * 512, (ts + 1) * 512)
            pnm, pinv = P[5], P[3]
            E("tensor", lambda e, pnm=pnm, ts=ts:
              e.matmul(pnm[:, :], onesr_f[0:1, :],
                       nm_r[:, ts * 512:(ts + 1) * 512],
                       start=True, stop=True),
              waits=[("VE", ve_st1)] + pw(pnm), inc="MM")
            mmb = E("tensor", lambda e, pinv=pinv, ts=ts:
                    e.matmul(pinv[:, :], onesr_f[0:1, :],
                             inv_r[:, ts * 512:(ts + 1) * 512],
                             start=True, stop=True),
                    waits=pw(pinv), inc="MM")
            for oc in range(KC):
                xt_ = x1t[oc % 2]
                lx = dma("scalar", "LDA", xt_,
                         x1_dram[oc * 128:(oc + 1) * 128, tsl],
                         waits=[("STS", x1_store[(oc, ts)])] + pw(xt_),
                         echo=True)
                xl = xlnt[oc % 2]
                E("vector", lambda e, xt_=xt_, xl=xl, pnm=pnm:
                  e.tensor_tensor(xl, xt_, pnm[:, :], op=ALU.add),
                  waits=[("LDA", lx), ("MM", mmb)] + pw(xl))
                vsep()
                ve = E("vector", lambda e, xl=xl, pinv=pinv:
                       e.tensor_tensor(xl, xl, pinv[:, :], op=ALU.mult),
                       inc="VE")
                setfree(xt_, "VE", ve)
                vv = vfence()
                st_ = dma("sync", "STS",
                          xln_dram[oc * 128:(oc + 1) * 128, tsl], xl,
                          waits=[("VE", vv)])
                E("vector", lambda e, xl=xl, oc=oc, tsl=tsl:
                  e.tensor_copy(xln16[oc][:, tsl], xl), inc="VE")
                pg = P[2] if ts == 0 else P[4]
                last_g = E("tensor",
                           lambda e, xl=xl, oc=oc, pg=pg:
                           e.matmul(pg[0:8, :],
                                    wg_s[:, oc * 8:(oc + 1) * 8],
                                    xl,
                                    start=(oc == 0), stop=(oc == 15)),
                           waits=([("VE", vv)]
                                  + (pw(pg) if oc == 0 else [])),
                           inc="MM")
                setfree(xl, "MM", last_g)
                addfree(xl, "STS", st_)
            setfree(pnm, "VE", p.cnt["VE"])
            setfree(pinv, "VE", p.cnt["VE"])
        sts_xln = p.cnt["STS"]
        mm_gate = p.cnt["MM"]
        for ts in range(2):
            pg = P[2] if ts == 0 else P[4]
            ve = E("vector", lambda e, ts=ts, pg=pg:
                   e.tensor_scalar(gt_sb[:, ts * 512:(ts + 1) * 512],
                                   pg[0:8, :], bg_s[0:8, 0:1], None,
                                   op0=ALU.add),
                   waits=[("MM", mm_gate), ("LDS", ld_consts)], inc="VE")
            setfree(pg, "VE", ve)
        ve_gt = vfence()
        for tt8 in range(8):
            pt2g = P[2 + (tt8 % 2)]
            mmg = E("tensor", lambda e, tt8=tt8, pt2g=pt2g:
                    e.transpose(pt2g[:, 0:8],
                                gt_sb[:, tt8 * 128:(tt8 + 1) * 128],
                                idf_s[0:8, 0:8]),
                    waits=[("VE", ve_gt), ("LDS", ld_consts)] + pw(pt2g),
                    inc="MM")
            ve = E("vector", lambda e, tt8=tt8, pt2g=pt2g:
                   e.tensor_copy(gate_s[:, tt8 * 8:(tt8 + 1) * 8],
                                 pt2g[:, 0:8]),
                   waits=[("MM", mmg)], inc="VE")
            setfree(pt2g, "VE", ve)

        GB = 64

        def gc(t, a, b):
            return scr[:, t * GB + a:t * GB + b]

        for t in range(8):
            E("vector", lambda e, t=t:
              e.tensor_scalar(gc(t, 0, 8), gate_s[:, t * 8:(t + 1) * 8],
                              -1.0, None, op0=ALU.mult))
        for t in range(8):
            E("vector", lambda e, t=t:
              e.reduce_max(gc(t, 48, 49), gc(t, 0, 8), axis=AX.X))
        for t in range(8):
            E("vector", lambda e, t=t:
              e.tensor_scalar(gc(t, 8, 16), gc(t, 0, 8), gc(t, 48, 49),
                              0.0, op0=ALU.subtract, op1=ALU.is_equal))
        for t in range(8):
            E("vector", lambda e, t=t:
              e.scalar_tensor_tensor(gc(t, 16, 24), gc(t, 8, 16), -1.0e30,
                                     gc(t, 0, 8), op0=ALU.mult, op1=ALU.add))
        for t in range(8):
            E("vector", lambda e, t=t:
              e.reduce_max(gc(t, 49, 50), gc(t, 16, 24), axis=AX.X))
        for t in range(8):
            E("vector", lambda e, t=t:
              e.tensor_scalar(gc(t, 24, 32), gc(t, 16, 24), gc(t, 49, 50),
                              0.0, op0=ALU.subtract, op1=ALU.is_equal))
        for t in range(8):
            E("vector", lambda e, t=t:
              e.tensor_tensor(gc(t, 32, 40), gc(t, 8, 16), gc(t, 24, 32),
                              op=ALU.add))
        for t in range(8):
            E("vector", lambda e, t=t:
              e.tensor_scalar(gc(t, 32, 40), gc(t, 32, 40), -1.0, 1.0,
                              op0=ALU.mult, op1=ALU.add))
        ve_nm = None
        for t in range(8):
            ve_nm = E("vector", lambda e, t=t:
                      e.reduce_max(gc(t, 50, 51),
                                   gate_s[:, t * 8:(t + 1) * 8],
                                   axis=AX.X, negate=True), inc="VE")
        sc_exp = None
        for t in range(8):
            sc_exp = E("scalar", lambda e, t=t:
                       e.activation(gc(t, 40, 48),
                                    gate_s[:, t * 8:(t + 1) * 8], AF.Exp,
                                    bias=gc(t, 50, 51), scale=1.0),
                       waits=[("VE", ve_nm)], inc="SC")
        for t in range(8):
            E("vector", lambda e, t=t:
              e.tensor_tensor(gc(t, 40, 48), gc(t, 40, 48), gc(t, 32, 40),
                              op=ALU.mult), waits=[("SC", sc_exp)])
        for t in range(8):
            E("vector", lambda e, t=t:
              e.reduce_sum(gc(t, 51, 52), gc(t, 40, 48), axis=AX.X))
        for t in range(8):
            E("vector", lambda e, t=t:
              e.reciprocal(gc(t, 52, 53), gc(t, 51, 52)))
        for t in range(8):
            E("vector", lambda e, t=t:
              e.tensor_scalar(w_s[:, t * 8:(t + 1) * 8], gc(t, 40, 48),
                              gc(t, 52, 53), None, op0=ALU.mult), inc="VE")

        for tt8 in range(8):
            cs = slice(tt8 * 8, tt8 * 8 + 8)
            E("vector", lambda e, cs=cs:
              e.tensor_copy(w128[:, 0:8], w_s[:, cs]),
              waits=pw(w128))
            ve2 = vfence()
            pt2 = P[2]
            mm = E("tensor", lambda e, pt2=pt2:
                   e.transpose(pt2[:, 0:128], w128[:, :], idf_s[:, :]),
                   waits=[("VE", ve2), ("LDS", ld_consts)] + pw(pt2),
                   inc="MM")
            setfree(w128, "MM", mm)
            ve3 = E("vector", lambda e, tt8=tt8, pt2=pt2:
                    e.tensor_copy(wT6[0:6, tt8 * 128:(tt8 + 1) * 128],
                                  pt2[0:6, 0:128]),
                    waits=[("MM", mm)], inc="VE")
            setfree(pt2, "VE", ve3)
        vsep()
        E("vector", lambda e: e.tensor_copy(wT6b[:, :], wT6[:, :]), inc="VE")
        ve_wT6 = vfence()
        setfree(scr, "SC", p.cnt["SC"])

        def hslot(slot, tc, hc):
            return hsf[slot * 16 + tc * 8 + hc]

        def sush(sc_i, tc):
            return hsf[sc_i * 2 + tc]

        state = {"mm_wave": None, "gbuf": 0}

        def gu_expert(n, slot):
            for tc in range(2):
                pb = P[4]
                mm = E("tensor", lambda e, pb=pb, n=n, tc=tc:
                       e.matmul(pb[:, :],
                                selh_s[0:6, n * 128:(n + 1) * 128],
                                wT6[0:6, tc * 512:(tc + 1) * 512],
                                start=True, stop=True),
                       waits=[("VE", ve_wT6), ("LDS", ld_consts)] + pw(pb),
                       inc="MM")
                ve = E("vector", lambda e, pb=pb, tc=tc:
                       e.tensor_copy(wBt[tc], pb[:, :]),
                       waits=[("MM", mm)] + pw(wBt[tc]), inc="VE")
                setfree(pb, "VE", ve)
            ve_wb = p.cnt["VE"]
            for q4 in range(4):
                hcs = (q4 * 2, q4 * 2 + 1)
                for which in ("g", "u"):
                    wt = Wg_img if which == "g" else Wu_img
                    gb = gslab[state["gbuf"] % 2]
                    state["gbuf"] += 1
                    lds = dma("gpsimd", "LDW", gb, wt[n, q4],
                              waits=pw(gb) + [("VE", ve_wT6)], echo=True)
                    for tc in range(2):
                        for hi, hc in enumerate(hcs):
                            pt = P[4 + ((tc * 2 + hi) % 4)]
                            mm = None
                            for kc in range(KC):
                                mm = E("tensor",
                                       lambda e, pt=pt, kc=kc, hi=hi, tc=tc,
                                       gb=gb:
                                       e.matmul(
                                           pt[:, :],
                                           gb[:, kc * 256 + hi * 128:
                                              kc * 256 + (hi + 1) * 128],
                                           xln16[kc][:,
                                                     tc * 512:(tc + 1) * 512],
                                           start=(kc == 0), stop=(kc == 15)),
                                       waits=([("LDW", lds),
                                               ("VE", ve_wT6)]
                                              + (pw(pt) if kc == 0 else [])),
                                       inc="MM" if kc == 15 else None)
                            if which == "g":
                                sgt = sg[tc * 2 + hi]
                                sc_ = E("scalar",
                                        lambda e, pt=pt, sgt=sgt, hc=hc, n=n:
                                        e.activation(
                                            sgt, pt[:, :], AF.Silu,
                                            bias=bgate_s[:, n * 8 + hc:
                                                         n * 8 + hc + 1],
                                            scale=1.0),
                                        waits=[("MM", mm)] + pw(sgt),
                                        inc="SC")
                                setfree(pt, "SC", sc_)
                                if tc == 1 and hi == 1:
                                    sfence()
                            else:
                                u_ = ut[hi]
                                sgt = sg[tc * 2 + hi]
                                ve = E("vector",
                                       lambda e, pt=pt, hc=hc, n=n, tc=tc,
                                       u_=u_:
                                       e.scalar_tensor_tensor(
                                           u_, pt[:, :],
                                           bup_s[:, n * 8 + hc:
                                                 n * 8 + hc + 1],
                                           wBt[tc],
                                           op0=ALU.add, op1=ALU.mult),
                                       waits=[("MM", mm), ("VE", ve_wb),
                                              ("SC", p.cnt["SC"])]
                                             + pw(u_), inc="VE")
                                setfree(pt, "VE", ve)
                                vsep()
                                hst = hslot(slot, tc, hc)
                                ve = E("vector",
                                       lambda e, hst=hst, sgt=sgt, u_=u_:
                                       e.tensor_tensor(hst, sgt, u_,
                                                       op=ALU.mult),
                                       waits=([("MM", state["mm_wave"])]
                                              if state["mm_wave"] else []),
                                       inc="VE")
                                setfree(u_, "VE", ve)
                                setfree(sgt, "VE", ve)
                    setfree(gb, "MM", p.cnt["MM"])

        def down_wave(wave, ns):
            ve_hs = vfence()
            for dg2 in range(8):
                for i, n in enumerate(ns):
                    rt = rtw[(dg2 * 2 + i) % 2]
                    ldr = dma("gpsimd", "LDW", rt, Wdn_img[dg2, n],
                              waits=pw(rt) + [("VE", ve_wT6)], echo=True)
                    mmlast = None
                    for dc in range(2):
                        for tc in range(2):
                            pdn = P[dc * 2 + tc]
                            slot = n % 2
                            for hc in range(HC):
                                st_mm = (i == 0 and hc == 0)
                                mmlast = E(
                                    "tensor",
                                    lambda e, pdn=pdn, rt=rt, dc=dc, tc=tc,
                                    hc=hc, slot=slot, st_mm=st_mm:
                                    e.matmul(
                                        pdn[:, :],
                                        rt[:, hc * 256 + dc * 128:
                                           hc * 256 + dc * 128 + 128],
                                        hslot(slot, tc, hc),
                                        start=(st_mm and wave == 0) or
                                              (st_mm and wave > 0),
                                        stop=False),
                                    waits=([("LDW", ldr), ("VE", ve_hs)]
                                           + (pw(pdn) if st_mm else [])),
                                    inc="MM" if hc == HC - 1 else None)
                    setfree(rt, "MM", mmlast)
                for dc in range(2):
                    dchunk = dg2 * 2 + dc
                    for tc in range(2):
                        pdn = P[dc * 2 + tc]
                        if wave == 0:
                            mm = E("tensor",
                                   lambda e, pdn=pdn, dchunk=dchunk, tc=tc:
                                   e.matmul(
                                       pdn[:, :],
                                       bdn_s[0:6,
                                             dchunk * 128:(dchunk + 1) * 128],
                                       wT6b[0:6, tc * 512:(tc + 1) * 512],
                                       start=False, stop=True),
                                   waits=[("VE", ve_wT6),
                                          ("LDW", ld_bdn)], inc="MM")
                            ve = E("vector",
                                   lambda e, pdn=pdn, dchunk=dchunk, tc=tc:
                                   e.tensor_copy(
                                       partial[dchunk][:,
                                                       tc * 512:
                                                       (tc + 1) * 512],
                                       pdn[:, :]),
                                   waits=[("MM", mm)], inc="VE")
                        else:
                            ve = E("vector",
                                   lambda e, pdn=pdn, dchunk=dchunk, tc=tc:
                                   e.tensor_tensor(
                                       partial[dchunk][:,
                                                       tc * 512:
                                                       (tc + 1) * 512],
                                       pdn[:, :],
                                       partial[dchunk][:,
                                                       tc * 512:
                                                       (tc + 1) * 512],
                                       op=ALU.add),
                                   waits=[("MM", p.cnt["MM"])], inc="VE")
                        setfree(pdn, "VE", ve)
            state["mm_wave"] = p.cnt["MM"]

        for wv, ns in enumerate(((0, 1), (2, 3), (4, 5))):
            for i, n in enumerate(ns):
                gu_expert(n, i)
            down_wave(wv, ns)

        sc_sup = None
        for q8 in range(8):
            gb = gslab[state["gbuf"] % 2]
            state["gbuf"] += 1
            lds = dma("gpsimd", "LDW", gb, Ws_img[q8],
                      waits=pw(gb) + [("VE", ve_wT6)], echo=True)
            for tc in range(2):
                for si in range(2):
                    sc_i = q8 * 2 + si
                    pt = P[4 + ((tc * 2 + si) % 4)]
                    mm = None
                    for kc in range(KC):
                        mm = E("tensor", lambda e, pt=pt, kc=kc, si=si,
                               tc=tc, gb=gb:
                               e.matmul(pt[:, :],
                                        gb[:, kc * 256 + si * 128:
                                           kc * 256 + (si + 1) * 128],
                                        xln16[kc][:,
                                                  tc * 512:(tc + 1) * 512],
                                        start=(kc == 0), stop=(kc == 15)),
                               waits=([("LDW", lds)]
                                      + (pw(pt) if kc == 0 else [])),
                               inc="MM" if kc == 15 else None)
                    sut = sush(sc_i, tc)
                    sc_sup = E("scalar",
                               lambda e, pt=pt, sut=sut, sc_i=sc_i:
                               e.activation(sut, pt[:, :], AF.Silu,
                                            bias=bsup_s[:, sc_i:sc_i + 1],
                                            scale=1.0),
                               waits=[("MM", mm)] + pw(sut), inc="SC")
                    setfree(pt, "SC", sc_sup)
                    setfree(sut, "SC", sc_sup)
            setfree(gb, "MM", p.cnt["MM"])
        sc_sup = sfence()

        x2_store = {}
        ve_waves = p.cnt["VE"]
        for dg2 in range(8):
            st0 = stw[dg2 % 2]
            lh1 = dma("gpsimd", "LDW", st0, Wsdn_img[dg2],
                      waits=pw(st0) + [("SC", sc_sup), ("VE", ve_waves)],
                      echo=True)
            for dc in range(2):
                dchunk = dg2 * 2 + dc
                for tc in range(2):
                    pdn = P[dc * 2 + tc]
                    mm = None
                    for sc_i in range(KC):
                        mm = E("tensor",
                               lambda e, pdn=pdn, sc_i=sc_i, dc=dc, tc=tc,
                               st0=st0:
                               e.matmul(
                                   pdn[:, :],
                                   st0[:, sc_i * 256 + dc * 128:
                                       sc_i * 256 + dc * 128 + 128],
                                   sush(sc_i, tc),
                                   start=(sc_i == 0), stop=(sc_i == 15)),
                               waits=([("LDW", lh1), ("SC", sc_sup)]
                                      + (pw(pdn) if sc_i == 0 else [])),
                               inc="MM" if sc_i == 15 else None)
                    tsl = slice(tc * 512, (tc + 1) * 512)
                    xl_ = xlt[dchunk % 2]
                    lx = dma("scalar", "LDA", xl_,
                             xln_dram[dchunk * 128:(dchunk + 1) * 128, tsl],
                             waits=pw(xl_) + [("STS", sts_xln)], echo=True)
                    x2_ = x2t[dchunk % 2]
                    E("vector", lambda e, pdn=pdn, dchunk=dchunk, tc=tc,
                      x2_=x2_:
                      e.scalar_tensor_tensor(
                          x2_, pdn[:, :], bsdn_s[:, dchunk:dchunk + 1],
                          partial[dchunk][:, tc * 512:(tc + 1) * 512],
                          op0=ALU.add, op1=ALU.add),
                      waits=[("MM", mm)] + pw(x2_))
                    vsep()
                    ve = E("vector", lambda e, x2_=x2_, xl_=xl_:
                           e.tensor_tensor(x2_, x2_, xl_, op=ALU.add),
                           waits=[("LDA", lx)], inc="VE")
                    setfree(pdn, "VE", ve)
                    setfree(xl_, "VE", ve)
                    vv = vfence()
                    st_ = dma("sync", "STS",
                              x2_dram[dchunk * 128:(dchunk + 1) * 128, tsl],
                              x2_, waits=[("VE", vv)])
                    x2_store[(dchunk, tc)] = st_
                    sq = sqf[dchunk % 2]
                    sc_ = E("scalar", lambda e, x2_=x2_, sq=sq:
                            e.activation(sq, x2_, AF.Square),
                            waits=[("VE", vv)] + pw(sq), inc="SC")
                    sc2 = sfence()
                    pssum = P[4 + tc]
                    pssq = P[6 + tc]
                    first = (dg2 == 0 and dc == 0)
                    mm1 = E("tensor", lambda e, x2_=x2_, pssum=pssum,
                            first=first:
                            e.matmul(pssum[0:1, :], onesc_f[:, 0:1],
                                     x2_, start=first, stop=False),
                            waits=([("VE", vv)]
                                   + (pw(pssum) if first else [])),
                            inc="MM")
                    setfree(x2_, "MM", mm1)
                    addfree(x2_, "SC", sc_)
                    addfree(x2_, "STS", st_)
                    mm2 = E("tensor", lambda e, sq=sq, pssq=pssq,
                            first=first:
                            e.matmul(pssq[0:1, :], onesc_f[:, 0:1],
                                     sq, start=first, stop=False),
                            waits=([("SC", sc2)]
                                   + (pw(pssq) if first else [])),
                            inc="MM")
                    setfree(sq, "MM", mm2)
            setfree(st0, "MM", p.cnt["MM"])
        mm_ln2 = p.cnt["MM"]
        for tc in range(2):
            ve = E("vector", lambda e, tc=tc:
                   e.tensor_copy(nm_r[:, tc * 512:(tc + 1) * 512],
                                 P[4 + tc][0:1, :]),
                   waits=[("MM", mm_ln2)], inc="VE")
            setfree(P[4 + tc], "VE", ve)
            ve = E("vector", lambda e, tc=tc:
                   e.tensor_copy(inv_r[:, tc * 512:(tc + 1) * 512],
                                 P[6 + tc][0:1, :]), inc="VE")
            setfree(P[6 + tc], "VE", ve)
        ve_st2 = ln_stats()
        sts_out = None
        for ts in range(2):
            tsl = slice(ts * 512, (ts + 1) * 512)
            pnm, pinv = P[5], P[3]
            E("tensor", lambda e, pnm=pnm, ts=ts:
              e.matmul(pnm[:, :], onesr_f[0:1, :],
                       nm_r[:, ts * 512:(ts + 1) * 512],
                       start=True, stop=True),
              waits=[("VE", ve_st2)] + pw(pnm), inc="MM")
            mmb = E("tensor", lambda e, pinv=pinv, ts=ts:
                    e.matmul(pinv[:, :], onesr_f[0:1, :],
                             inv_r[:, ts * 512:(ts + 1) * 512],
                             start=True, stop=True),
                    waits=pw(pinv), inc="MM")
            for oc in range(KC):
                x2_ = x2t[oc % 2]
                lx = dma("scalar", "LDA", x2_,
                         x2_dram[oc * 128:(oc + 1) * 128, tsl],
                         waits=[("STS", x2_store[(oc, ts)])] + pw(x2_),
                         echo=True)
                x3_ = x3t[oc % 2]
                E("vector", lambda e, x2_=x2_, x3_=x3_, pnm=pnm:
                  e.tensor_tensor(x3_, x2_, pnm[:, :], op=ALU.add),
                  waits=[("LDA", lx), ("MM", mmb)] + pw(x3_))
                vsep()
                ve = E("vector", lambda e, x3_=x3_, pinv=pinv:
                       e.tensor_tensor(x3_, x3_, pinv[:, :], op=ALU.mult),
                       inc="VE")
                setfree(x2_, "VE", ve)
                vv = vfence()
                sts_out = dma("sync", "STS",
                              out[oc * 128:(oc + 1) * 128, tsl],
                              x3_, waits=[("VE", vv)])
                setfree(x3_, "STS", sts_out)
            setfree(pnm, "VE", p.cnt["VE"])
            setfree(pinv, "VE", p.cnt["VE"])
        E("sync", lambda e: None, waits=[("STS", sts_out)])

        def runner(name):
            def run(eng):
                for need, fn, inc, n in p.steps[name]:
                    for s, v in need:
                        eng.wait_ge(sem[s], v)
                    if fn is None:
                        continue
                    inst = fn(eng)
                    if inc is not None:
                        inst.then_inc(sem[inc], n)
            return run

        block.sync(runner("sync"))
        block.gpsimd(runner("gpsimd"))
        block.tensor(runner("tensor"))
        block.vector(runner("vector"))
        block.scalar(runner("scalar"))

    return nc


_NC_CACHE = {}


def _build_cached():
    if "nc" not in _NC_CACHE:
        _NC_CACHE["nc"] = build_nc()
    return _NC_CACHE["nc"]


def _rope_tables():
    inv = 1.0 / (10000.0 ** (np.arange(0, HD, 2, dtype=np.float64) / HD))
    t = np.arange(S, dtype=np.float64)
    fr = t[:, None] * inv[None, :]
    emb = np.concatenate([fr, fr], axis=1)
    cos = np.cos(emb).T.astype(np.float32)
    sin = np.sin(emb).T.astype(np.float32)
    return cos, sin


def _cols(vec, n):
    return np.ascontiguousarray(
        np.asarray(vec).reshape(n, 128).T.astype(np.float32))


def kernel(**inputs):
    f = {k: np.asarray(v) for k, v in inputs.items()}
    x = f["L0_attn_hidden_states"].astype(np.float32, copy=False)
    wq, bq = f["L0_attn_wq"], f["L0_attn_bq"]
    wk, bk = f["L0_attn_wk"], f["L0_attn_bk"]
    wv, bv = f["L0_attn_wv"], f["L0_attn_bv"]
    wo, bo = f["L0_attn_wo"], f["L0_attn_bo"]

    cos, sin = _rope_tables()
    cosq = (cos * SCALE).astype(BF)
    sinq = (sin * SCALE).astype(BF)
    cosk = cos.astype(BF)
    sink = sin.astype(BF)
    rsT = np.zeros((128, 128), dtype=np.float32)
    for i_ in range(64):
        rsT[64 + i_, i_] = -1.0
        rsT[i_, 64 + i_] = 1.0
    rsT = rsT.astype(BF)
    dm = np.zeros((128, 4 * 512), dtype=np.float32)
    ii = np.arange(128)[:, None]
    jj = np.arange(512)[None, :]
    for m in range(4):
        dm[:, m * 512:(m + 1) * 512] = (m * 128 + ii <= jj)
    dmask = dm.astype(BF)
    idf = np.eye(128, dtype=np.float32)
    selh = np.zeros((NR, NR * 128), dtype=np.float32)
    for n in range(NR):
        selh[n, n * 128:(n + 1) * 128] = 1.0

    woT = np.ascontiguousarray(wo.T)

    def img16(w2d, qw):
        m = w2d.shape[1]
        return np.ascontiguousarray(
            w2d.reshape(KC, 128, m // qw, qw).transpose(2, 1, 0, 3).reshape(
                m // qw, 128, KC * qw).astype(np.float32))

    Wdn_img = np.ascontiguousarray(
        f["L0_ffn_Wdown"].reshape(NR, HC, 128, 8, 256).transpose(
            3, 0, 2, 1, 4).reshape(8, NR, 128, 2048).astype(np.float32))
    Wsdn_img = img16(f["L0_ffn_Ws_down"], 256)
    Wg2 = np.ascontiguousarray(
        f["L0_ffn_Wg"].reshape(KC, 128, 8).transpose(1, 0, 2).reshape(128, 128))

    common = dict(
        Wg2=Wg2, bgr=f["L0_ffn_bg"].reshape(8, 1).astype(np.float32),
        Wg_img=np.stack([img16(f["L0_ffn_Wgate"][n], 256)
                         for n in range(NR)]),
        Wu_img=np.stack([img16(f["L0_ffn_Wup"][n], 256)
                         for n in range(NR)]),
        bgatec=_cols(f["L0_ffn_bgate"].reshape(-1), NR * HC),
        bupc=_cols(f["L0_ffn_bup"].reshape(-1), NR * HC),
        Wdn_img=Wdn_img, bdn=f["L0_ffn_bdown"].astype(np.float32),
        Ws_img=img16(f["L0_ffn_Ws_up"].astype(np.float32), 256),
        bsupc=_cols(f["L0_ffn_bs_up"], KC),
        Wsdn_img=Wsdn_img, bsdnc=_cols(f["L0_ffn_bs_down"], KC),
        boc=_cols(bo, KC),
        cosq=cosq, sinq=sinq, cosk=cosk, sink=sink, dmask=dmask, idf=idf,
        selh=selh, rsT=rsT,
    )

    in_maps = []
    for c in range(8):
        b, par = c // 2, c % 2
        hsl = slice(par * 1024, (par + 1) * 1024)
        psl = slice((1 - par) * 1024, (2 - par) * 1024)
        sel = np.zeros((128, 4), dtype=np.float32)
        if par == 0:
            sel[:, 1] = 1.0
            sel[:, 2] = 1.0
        else:
            sel[:, 0] = 1.0
            sel[:, 3] = 1.0
        m = dict(common)
        m.update(
            xT=np.ascontiguousarray(x[b].T),
            xownT=np.ascontiguousarray(x[b, hsl].T),
            wq_img=img16(np.ascontiguousarray(wq[hsl].T.astype(np.float32)), 256),
            wk_img=img16(np.ascontiguousarray(wk[hsl].T.astype(np.float32)), 256),
            wv_img=img16(np.ascontiguousarray(wv[hsl].T.astype(np.float32)), 256),
            woTl=np.ascontiguousarray(
                np.concatenate([woT[hsl], woT[psl]], axis=0).astype(
                    np.float32)),
            bqc=_cols(bq[hsl], NHL), bkc=_cols(bk[hsl], NHL),
            bvr=np.asarray(bv[hsl]).reshape(1, -1).astype(np.float32),
            selc=sel,
        )
        in_maps.append(m)

    nc = _build_cached()
    res = run_bass_kernel_spmd(nc, in_maps, core_ids=list(range(8)))

    y = np.empty((B, S, DM), dtype=np.float32)
    for c in range(8):
        b, par = c // 2, c % 2
        y[b, par * 1024:(par + 1) * 1024, :] = res.results[c]["out"].T
    return y


# version 16
# speedup vs baseline: 1.0791x; 1.0464x over previous
import numpy as np
import ml_dtypes
from contextlib import ExitStack

import concourse.bass as bass
import concourse.mybir as mybir
from concourse.bass_utils import run_bass_kernel_spmd

F32 = mybir.dt.float32
BF16 = mybir.dt.bfloat16
AF = mybir.ActivationFunctionType
ALU = mybir.AluOpType
AX = mybir.AxisListType
BF = ml_dtypes.bfloat16

B, S, DM = 4, 2048, 2048
NHL, HD, GRP, KC = 8, 128, 2, 16
TOK = 1024
NR, H, HC = 6, 1024, 8
SCALE = HD ** -0.5
EPS = 1e-5

ENGINES = ("sync", "gpsimd", "tensor", "vector", "scalar")
SEMS = ("LDW", "LDS", "STS", "MM", "VE", "SC", "CC", "LDA")


class Prog:
    def __init__(self):
        self.steps = {e: [] for e in ENGINES}
        self.cnt = {s: 0 for s in SEMS}
        self.ensured = {e: {s: 0 for s in SEMS} for e in ENGINES}

    def emit(self, eng, fn, waits=(), inc=None, n=1):
        need = []
        for s, v in waits:
            if v is None:
                continue
            assert v <= self.cnt[s], f"wait {s}>={v} > emitted {self.cnt[s]}"
            if v > self.ensured[eng][s]:
                need.append((s, v))
                self.ensured[eng][s] = v
        if inc is not None:
            self.cnt[inc] += n
        self.steps[eng].append((tuple(need), fn, inc, n))
        return self.cnt[inc] if inc is not None else None


class View:

    def __init__(self, arena, off, n, w):
        self.a, self.off, self.n, self.w = arena, off, n, w

    def __getitem__(self, i):
        assert 0 <= i < self.n
        if not hasattr(self, "_c"):
            self._c = {}
        if i not in self._c:
            o = self.off + i * self.w
            self._c[i] = self.a[:, o:o + self.w]
        return self._c[i]


def build_nc():
    nc = bass.Bass("TRN2", target_bir_lowering=False)
    d = {}

    def inp(name, shape, dt=F32):
        d[name] = nc.dram_tensor(name, list(shape), dt, kind="ExternalInput").ap()
        return d[name]

    xT = inp("xT", (DM, S))
    xownT = inp("xownT", (DM, TOK))
    wq_img = inp("wq_img", (4, 128, 4096))
    wk_img = inp("wk_img", (4, 128, 4096))
    wv_img = inp("wv_img", (4, 128, 4096))
    woTl = inp("woTl", (DM, DM))
    bqc = inp("bqc", (128, NHL))
    bkc = inp("bkc", (128, NHL))
    bvr = inp("bvr", (1, NHL * HD))
    boc = inp("boc", (128, KC))
    Wg2 = inp("Wg2", (128, 128))
    bgr = inp("bgr", (8, 1))
    Wg_img = inp("Wg_img", (NR, 4, 128, 4096))
    Wu_img = inp("Wu_img", (NR, 4, 128, 4096))
    bgatec = inp("bgatec", (128, NR * HC))
    bupc = inp("bupc", (128, NR * HC))
    Wdn_img = inp("Wdn_img", (8, NR, 128, 2048))
    bdn = inp("bdn", (NR, DM))
    Ws_img = inp("Ws_img", (8, 128, 4096))
    bsupc = inp("bsupc", (128, KC))
    Wsdn_img = inp("Wsdn_img", (8, 128, 4096))
    bsdnc = inp("bsdnc", (128, KC))
    cosq_d = inp("cosq", (128, S), BF16)
    sinq_d = inp("sinq", (128, S), BF16)
    cosk_d = inp("cosk", (128, S), BF16)
    sink_d = inp("sink", (128, S), BF16)
    dmask = inp("dmask", (128, 4 * 512), BF16)
    idf = inp("idf", (128, 128))
    selc = inp("selc", (128, 4))
    selh = inp("selh", (NR, NR * 128))
    rsT_d = inp("rsT", (128, 128), BF16)

    out = nc.dram_tensor("out", [DM, TOK], F32, kind="ExternalOutput").ap()

    cc_in_a = nc.dram_tensor("cc_in_a", [4 * HD, TOK], BF16).ap()
    cc_out_a = nc.dram_tensor("cc_out_a", [8 * HD, TOK], BF16).ap()
    cc_in_b = nc.dram_tensor("cc_in_b", [4 * HD, TOK], BF16).ap()
    cc_out_b = nc.dram_tensor("cc_out_b", [8 * HD, TOK], BF16).ap()
    x1_dram = nc.dram_tensor("x1_dram", [DM, TOK], F32).ap()
    xln_dram = nc.dram_tensor("xln_dram", [DM, TOK], F32).ap()
    x2_dram = nc.dram_tensor("x2_dram", [DM, TOK], F32).ap()
    sep_dram = nc.dram_tensor("sep_dram", [1, 64], F32).ap()

    p = Prog()
    E = p.emit
    es = ExitStack()

    def sbt(name, shape, dt):
        return es.enter_context(nc.sbuf_tensor(name, list(shape), dt))

    with es:
        AXa = sbt("AXa", (128, 32768), BF16)
        AYa = sbt("AYa", (128, 16384), BF16)
        AWa = sbt("AWa", (128, 11264), BF16)
        AVa = sbt("AVa", (128, 4096), BF16)
        AFa = sbt("AFa", (128, 7168), F32)

        xs = View(AXa, 0, KC, S)
        wo_slab = View(AXa, 0, KC, DM)
        xln16 = View(AXa, 0, KC, TOK)
        partial = View(AXa, KC * TOK, KC, TOK)
        qrot = View(AYa, 0, GRP, S)
        krot = View(AYa, GRP * S, GRP, S)
        vtm = View(AYa, 2 * GRP * S, KC, 256)
        wslab = View(AYa, 2 * GRP * S + KC * 256, KC, 256)
        myown = View(AYa, 0, NHL, TOK)
        partner = View(AYa, NHL * TOK, NHL, TOK)
        hsf = View(AYa, 0, 32, 512)
        wslab2 = View(AWa, 0, KC, 256)
        c_cosk = View(AWa, 2 * S, 1, S)[0]
        c_sink = View(AWa, 3 * S, 1, S)[0]
        maskt = View(AWa, 4 * S, 4, 512)
        rcv = View(AWa, 0, 4, TOK)
        sendt = View(AWa, 4 * TOK, 2, TOK)
        gslab = View(AWa, 0, 2, 4096)
        sg = View(AWa, 8192, 4, 512)
        ut = View(AWa, 8192 + 4 * 512, 2, 512)
        stw = View(AWa, 0, 2, 4096)
        probs = View(AVa, 0, 3, 512)
        sq512 = View(AVa, 3 * 512, 1, 512)[0]
        rope16 = View(AVa, 2048, 2, 512)
        sendE = View(AVa, 3072, 1, 1024)[0]
        rtw = View(AVa, 0, 2, HC * 256)
        ropet = View(AFa, 0, 3, 512)
        rB = View(AFa, 3 * 512, 1, 512)[0]
        ssum_t = AFa[0:1, 2048:2560]
        xot = View(AFa, 0, 2, 512)
        x1t = View(AFa, 2 * 512, 2, 512)
        sqf = View(AFa, 4 * 512, 2, 512)
        xlnt = View(AFa, 6 * 512, 2, 512)
        wBt = View(AFa, 0, 2, 512)
        x2t = View(AFa, 2 * 512, 2, 512)
        x3t = View(AFa, 6 * 512, 2, 512)
        xlt = View(AFa, 6 * 512, 2, 512)
        nm_r = AFa[0:1, 4096:5120]
        inv_r = AFa[0:1, 5120:6144]
        tmp_r = AFa[0:1, 6144:7168]
        gt_sb = AFa[0:8, 6144:7168]

        attnfm = sbt("attnfm", (128, NHL * S), BF16)
        atv = View(attnfm, 0, NHL, S)
        idf_s = sbt("idf_s", (128, 128), F32)
        selc_s = sbt("selc_s", (128, 4), F32)
        bq_s = sbt("bq_s", (128, NHL), F32)
        bk_s = sbt("bk_s", (128, NHL), F32)
        bv_s = sbt("bv_s", (1, NHL * HD), BF16)
        bo_s = sbt("bo_s", (128, KC), F32)
        wg_s = sbt("wg_s", (128, 128), F32)
        bg_s = sbt("bg_s", (8, 1), F32)
        bgate_s = sbt("bgate_s", (128, NR * HC), F32)
        bup_s = sbt("bup_s", (128, NR * HC), F32)
        bdn_s = sbt("bdn_s", (NR, DM), BF16)
        wT6b = sbt("wT6b", (NR, TOK), BF16)
        bsup_s = sbt("bsup_s", (128, KC), F32)
        bsdn_s = sbt("bsdn_s", (128, KC), F32)
        ones_b = sbt("ones_b", (128, 1), BF16)
        onesr_b = sbt("onesr_b", (1, 128), BF16)
        onesr_f = sbt("onesr_f", (1, 128), F32)
        onesc_f = sbt("onesc_f", (128, 1), F32)
        scr = sbt("scr", (128, 512), F32)
        gate_s = sbt("gate_s", (128, 64), F32)
        w_s = sbt("w_s", (128, 64), F32)
        w128 = sbt("w128", (128, 128), F32)
        wT6 = sbt("wT6", (NR, TOK), F32)
        negb = sbt("negb", (128, 1), F32)
        selh_s = sbt("selh_s", (NR, NR * 128), F32)
        rsT_s = sbt("rsT_s", (128, 128), BF16)
        eps_s = sbt("eps_s", (1, 1), F32)
        sep16 = sbt("sep16", (128, 16), BF16)
        sepd_s = sbt("sepd_s", (1, 64), F32)

        P = [es.enter_context(nc.psum_tensor(f"P{i}", [128, 512], F32))
             for i in range(8)]
        sem = {s: es.enter_context(nc.semaphore(s)) for s in SEMS}
        block = es.enter_context(nc.Block())

        pfree = {}

        def key(ap_or_obj):
            return id(ap_or_obj)

        def pw(*objs):
            w = []
            for t in objs:
                w.extend(pfree.get(key(t), []))
            return w

        def setfree(obj, semname, val):
            pfree[key(obj)] = [(semname, val)]

        def addfree(obj, semname, val):
            pfree.setdefault(key(obj), []).append((semname, val))

        def vsep():
            E("vector", lambda e:
              e.tensor_copy(sep16[:, 0:8], sep16[:, 8:16]))

        def vfence():
            vsep()
            vsep()
            return E("vector", lambda e:
                     e.tensor_copy(sep16[:, 0:8], sep16[:, 8:16]), inc="VE")

        def sfence():
            E("scalar", lambda e:
              e.activation(sep16[:, 0:8], sep16[:, 8:16], AF.Copy))
            return E("scalar", lambda e:
                     e.activation(sep16[:, 0:8], sep16[:, 8:16], AF.Copy),
                     inc="SC")

        def dma(eng, sm, outap, inap, waits=(), echo=False):
            r = E(eng, lambda e, o=outap, i=inap: e.dma_start(out=o, in_=i),
                  waits=waits, inc=sm, n=16)
            if echo:
                r = E(eng, lambda e: e.dma_start(out=sepd_s[:, :],
                                                 in_=sep_dram[:, :]),
                      inc=sm, n=16)
            return r

        for t_, src in ((c_cosk, cosk_d),
                        (c_sink, sink_d), (idf_s[:, :], idf),
                        (selc_s[:, :], selc), (bq_s[:, :], bqc),
                        (bk_s[:, :], bkc), (bo_s[:, :], boc),
                        (wg_s[:, :], Wg2), (bg_s[:, :], bgr),
                        (bgate_s[:, :], bgatec), (bup_s[:, :], bupc),
                        (bsup_s[:, :], bsupc),
                        (selh_s[:, :], selh), (rsT_s[:, :], rsT_d),
                        (bsdn_s[:, :], bsdnc)):
            dma("sync", "LDS", t_, src[:, :])
        for m in range(4):
            dma("sync", "LDS", maskt[m], dmask[:, m * 512:(m + 1) * 512])
        ld_consts = dma("sync", "LDS", sepd_s[:, :], sep_dram[:, :],
                        echo=True)
        ld_bv = dma("gpsimd", "LDW", bv_s[:, :], bvr[:, :], echo=True)
        ld_bdn = dma("gpsimd", "LDW", bdn_s[:, :], bdn[:, :], echo=True)

        ms_final = None
        for ap_, v in ((ones_b[:, :], 1.0), (onesr_b[:, :], 1.0),
                       (onesr_f[:, :], 1.0), (onesc_f[:, :], 1.0),
                       (eps_s[:, :], float(EPS)),
                       (negb[:, :], -20.0),
                       (w128[:, :], 0.0)):
            ms_final = E("vector", lambda e, a=ap_, vv=v: e.memset(a, vv),
                         inc="VE")
        ms_final = vfence()

        ldx = {}
        for kc in range(KC):
            ldx[kc] = dma("gpsimd", "LDW", xs[kc],
                          xT[kc * 128:(kc + 1) * 128, :], echo=True)

        for g in range(4):
            cols = slice(g * 256, (g + 1) * 256)
            for wsrc in ("q", "k", "v"):
                wd = {"q": wq_img, "k": wk_img, "v": wv_img}[wsrc]
                jbuf = (g * 3 + ("q", "k", "v").index(wsrc)) % 2
                wsl = wslab if jbuf == 0 else wslab2
                slab_ar = AYa if jbuf == 0 else AWa
                ldslab = dma("gpsimd", "LDW",
                             slab_ar[:, wsl.off:wsl.off + 4096], wd[g],
                             waits=pw(wsl) + ([("LDS", ld_consts)]
                                              if (g, wsrc) == (0, "k")
                                              else []),
                             echo=True)
                if wsrc in ("q", "k"):
                    bias = bq_s if wsrc == "q" else bk_s
                    cos = c_cosk
                    sinm = c_sink
                    dst = qrot if wsrc == "q" else krot
                    for hl in range(GRP):
                        h = g * GRP + hl
                        for sl in range(4):
                            pt = P[(0, 1, 5, 6)[sl % 4]]
                            mm = None
                            for kc in range(KC):
                                mm = E("tensor",
                                       lambda e, pt=pt, kc=kc, hl=hl, sl=sl,
                                       wsl=wsl:
                                       e.matmul(
                                           pt[:, :],
                                           wsl[kc][:,
                                                   hl * 128:(hl + 1) * 128],
                                           xs[kc][:, sl * 512:(sl + 1) * 512],
                                           start=(kc == 0), stop=(kc == 15)),
                                       waits=([("LDW", ldslab),
                                               ("LDW", ldx[kc])]
                                              + (pw(pt) if kc == 0 else [])),
                                       inc="MM" if kc == 15 else None)
                            dsl = slice(sl * 512, (sl + 1) * 512)
                            qb = rope16[sl % 2]
                            ve = E("vector",
                                   lambda e, pt=pt, h=h, bias=bias, qb=qb:
                                   e.tensor_scalar(qb, pt[:, :],
                                                   bias[:, h:h + 1], None,
                                                   op0=ALU.add),
                                   waits=[("MM", mm), ("LDS", ld_consts)]
                                         + pw(qb), inc="VE")
                            setfree(pt, "VE", ve)
                            prot = P[2 + (sl % 2)]
                            mmr = E("tensor", lambda e, prot=prot, qb=qb:
                                    e.matmul(prot[:, :], rsT_s[:, :], qb,
                                             start=True, stop=True),
                                    waits=[("VE", ve)] + pw(prot), inc="MM")
                            addfree(qb, "MM", mmr)
                            ve2 = E("vector",
                                    lambda e, dsl=dsl, sinm=sinm, prot=prot:
                                    e.tensor_tensor(ropet[1], prot[:, :],
                                                    sinm[:, dsl],
                                                    op=ALU.mult),
                                    waits=[("MM", mmr)], inc="VE")
                            setfree(prot, "VE", ve2)
                            E("vector", lambda e, dsl=dsl, cos=cos, qb=qb:
                              e.tensor_tensor(ropet[0], qb,
                                              cos[:, dsl], op=ALU.mult))
                            vsep()
                            ve = E("vector",
                                   lambda e, hl=hl, dsl=dsl, dst=dst:
                                   e.tensor_tensor(dst[hl][:, dsl], ropet[0],
                                                   ropet[1], op=ALU.add),
                                   inc="VE")
                            addfree(qb, "VE", ve)
                else:
                    for tcv in range(KC):
                        pt = P[(0, 1, 5, 6)[tcv % 4]]
                        for kc in range(KC):
                            E("tensor", lambda e, pt=pt, kc=kc, tcv=tcv,
                              wsl=wsl:
                              e.matmul(pt[:, 0:256],
                                       xs[kc][:, tcv * 128:(tcv + 1) * 128],
                                       wsl[kc],
                                       start=(kc == 0), stop=False),
                              waits=([("LDW", ldslab), ("LDW", ldx[kc])]
                                     + (pw(pt) if kc == 0 else [])),
                              inc=None)
                        mm = E("tensor", lambda e, pt=pt, g=g:
                               e.matmul(pt[:, 0:256], onesr_b[0:1, :],
                                        bv_s[0:1, g * 256:(g + 1) * 256],
                                        start=False, stop=True),
                               waits=[("VE", ms_final), ("LDW", ld_bv)],
                               inc="MM")
                        ve = E("vector", lambda e, pt=pt, tcv=tcv:
                               e.tensor_copy(vtm[tcv], pt[:, 0:256]),
                               waits=[("MM", mm)], inc="VE")
                        setfree(pt, "VE", ve)
                setfree(wsl, "MM", p.cnt["MM"])
            ve_qkv = vfence()
            if g == 3:
                mm_xs_free = p.cnt["MM"]

            for hl in range(GRP):
                h = g * GRP + hl
                for qs in range(4):
                    kcount = 4 * (qs + 1)
                    ppv = P[(0, 1, 2)[(g * GRP * 4 + hl * 4 + qs) % 3]]
                    psm = P[6 + ((g * GRP * 4 + hl * 4 + qs) % 2)]
                    qsl = slice(qs * 512, (qs + 1) * 512)
                    for kb in range(kcount):
                        pt = P[(3, 4)[kb % 2]]
                        mm = E("tensor", lambda e, pt=pt, hl=hl, kb=kb,
                               qsl=qsl:
                               e.matmul(pt[:, :],
                                        krot[hl][:, kb * 128:(kb + 1) * 128],
                                        qrot[hl][:, qsl],
                                        start=True, stop=True),
                               waits=[("VE", ve_qkv)] + pw(pt), inc="MM")
                        pr = probs[kb % 3]
                        sc_ = E("scalar", lambda e, pt=pt, pr=pr:
                                e.activation(pr, pt[:, :], AF.Exp,
                                             bias=negb[:, 0:1],
                                             scale=float(SCALE)),
                                waits=[("MM", mm), ("VE", ms_final)] + pw(pr),
                                inc="SC")
                        setfree(pt, "SC", sc_)
                        dep = ("SC", sc_)
                        m = kb - 4 * qs
                        if m >= 0:
                            vv = E("vector", lambda e, pr=pr, m=m:
                                   e.tensor_tensor(pr, pr, maskt[m],
                                                   op=ALU.mult),
                                   waits=[dep, ("LDS", ld_consts)], inc="VE")
                            dep = ("VE", vv)
                        first = (kb == 0)
                        last = (kb == kcount - 1)
                        E("tensor", lambda e, ppv=ppv, kb=kb, hl=hl, pr=pr,
                          first=first, last=last:
                          e.matmul(ppv[:, :],
                                   vtm[kb][:, hl * 128:(hl + 1) * 128],
                                   pr, start=first, stop=last),
                          waits=[dep] + (pw(ppv) if first else []),
                          inc="MM" if last else None)
                        E("tensor", lambda e, psm=psm, pr=pr,
                          first=first, last=last:
                          e.matmul(psm[0:1, :], ones_b[:, 0:1],
                                   pr, start=first, stop=last),
                          waits=(pw(psm) if first else []),
                          inc="MM" if last else None)
                        setfree(pr, "MM", p.cnt["MM"])
                    mm_all = p.cnt["MM"]
                    ve = E("vector", lambda e, psm=psm:
                           e.tensor_copy(ssum_t, psm[0:1, :]),
                           waits=[("MM", mm_all)] + pw(ssum_t), inc="VE")
                    setfree(psm, "VE", ve)
                    ve = E("vector", lambda e:
                           e.reciprocal(ssum_t, ssum_t), inc="VE")
                    pb = P[5]
                    mm = E("tensor", lambda e, pb=pb:
                           e.matmul(pb[:, :], onesr_f[0:1, :],
                                    ssum_t, start=True, stop=True),
                           waits=[("VE", ve)] + pw(pb), inc="MM")
                    setfree(ssum_t, "MM", mm)
                    ve = E("vector", lambda e, pb=pb:
                           e.tensor_copy(rB, pb[:, :]),
                           waits=[("MM", mm)], inc="VE")
                    setfree(pb, "VE", ve)
                    ve = E("vector", lambda e, h=h, qsl=qsl, ppv=ppv:
                           e.tensor_tensor(atv[h][:, qsl], ppv[:, :],
                                           rB, op=ALU.mult), inc="VE")
                    setfree(ppv, "VE", ve)

            if g in (1, 3):
                half = g // 2
                ccin = cc_in_a if half == 0 else cc_in_b
                for cl in range(4):
                    c = half * 4 + cl
                    sd = sendE
                    E("vector", lambda e, c=c, sd=sd:
                      e.tensor_scalar(sd, atv[c][:, 0:TOK],
                                      selc_s[:, 0:1], None, op0=ALU.mult),
                      waits=[("LDS", ld_consts)] + pw(sd))
                    vsep()
                    ve = E("vector", lambda e, c=c, sd=sd:
                           e.scalar_tensor_tensor(sd, atv[c][:, TOK:S],
                                                  selc_s[:, 1:2], sd,
                                                  op0=ALU.mult, op1=ALU.add),
                           inc="VE")
                    st_ = dma("sync", "STS",
                              ccin[cl * 128:(cl + 1) * 128, :], sd,
                              waits=[("VE", ve)])
                    setfree(sd, "STS", st_)
                if half == 0:
                    sts_send_a = p.cnt["STS"]
                else:
                    sts_send_b = p.cnt["STS"]

        mm_endB = p.cnt["MM"]
        ve_attn = p.cnt["VE"]
        sc_endB = p.cnt["SC"]

        for c in range(NHL):
            E("vector", lambda e, c=c:
              e.tensor_scalar(myown[c], atv[c][:, 0:TOK],
                              selc_s[:, 2:3], None, op0=ALU.mult),
              waits=[("VE", ve_attn), ("LDS", ld_consts), ("MM", mm_endB),
                     ("SC", sc_endB)])
        for c in range(NHL):
            E("vector", lambda e, c=c:
              e.scalar_tensor_tensor(myown[c], atv[c][:, TOK:S],
                                     selc_s[:, 3:4], myown[c],
                                     op0=ALU.mult, op1=ALU.add),
              inc="VE")
        for kc in range(KC):
            dma("gpsimd", "LDW", wo_slab[kc],
                woTl[kc * 128:(kc + 1) * 128, :],
                waits=[("MM", mm_xs_free)],
                echo=(kc == KC - 1))
        ld_wo = p.cnt["LDW"]
        E("gpsimd", None, waits=[("LDW", ld_wo)])
        cc = E("gpsimd", lambda e:
               e.collective_compute(
                   "AllGather", ALU.bypass,
                   replica_groups=[[0, 1], [2, 3], [4, 5], [6, 7]],
                   ins=[cc_in_a[:, :]], outs=[cc_out_a[:, :]]),
               waits=[("STS", sts_send_a)], inc="CC")
        cc = E("gpsimd", lambda e:
               e.collective_compute(
                   "AllGather", ALU.bypass,
                   replica_groups=[[0, 1], [2, 3], [4, 5], [6, 7]],
                   ins=[cc_in_b[:, :]], outs=[cc_out_b[:, :]]),
               waits=[("STS", sts_send_b)], inc="CC")
        for c in range(NHL):
            half = c // 4
            cl = c % 4
            ccout = cc_out_a if half == 0 else cc_out_b
            r0 = rcv[(c % 2) * 2]
            r1 = rcv[(c % 2) * 2 + 1]
            l0 = dma("sync", "LDS", r0, ccout[cl * 128:(cl + 1) * 128, :],
                     waits=[("CC", 1 + half), ("VE", ve_attn)] + pw(r0),
                     echo=True)
            l1 = dma("sync", "LDS", r1,
                     ccout[(4 + cl) * 128:(4 + cl + 1) * 128, :],
                     waits=pw(r1), echo=True)
            E("vector", lambda e, c=c, r0=r0:
              e.tensor_scalar(partner[c], r0, selc_s[:, 0:1], None,
                              op0=ALU.mult),
              waits=[("LDS", l0)])
            vsep()
            ve = E("vector", lambda e, c=c, r1=r1:
                   e.scalar_tensor_tensor(partner[c], r1, selc_s[:, 1:2],
                                          partner[c], op0=ALU.mult,
                                          op1=ALU.add),
                   waits=[("LDS", l1)], inc="VE")
            setfree(r0, "VE", ve)
            setfree(r1, "VE", ve)
        ve_part = vfence()

        x1_store = {}
        for ts in range(2):
            tsl = slice(ts * 512, (ts + 1) * 512)
            for oc in range(KC):
                pt = P[(0, 1, 2, 4)[oc % 4]]
                mm = None
                for kc in range(KC):
                    rhs = myown[kc] if kc < 8 else partner[kc - 8]
                    mm = E("tensor", lambda e, pt=pt, kc=kc, oc=oc, rhs=rhs,
                           tsl=tsl:
                           e.matmul(pt[:, :],
                                    wo_slab[kc][:, oc * 128:(oc + 1) * 128],
                                    rhs[:, tsl],
                                    start=(kc == 0), stop=(kc == 15)),
                           waits=([("LDW", ld_wo), ("VE", ve_part)]
                                  + (pw(pt) if kc == 0 else [])),
                           inc="MM" if kc == 15 else None)
                xo = xot[oc % 2]
                lx = dma("scalar", "LDA", xo,
                         xownT[oc * 128:(oc + 1) * 128, tsl],
                         waits=[("MM", mm_endB), ("VE", ve_attn)] + pw(xo),
                         echo=True)
                xt_ = x1t[oc % 2]
                ve = E("vector", lambda e, pt=pt, oc=oc, xo=xo, xt_=xt_:
                       e.scalar_tensor_tensor(xt_, pt[:, :],
                                              bo_s[:, oc:oc + 1], xo,
                                              op0=ALU.add, op1=ALU.add),
                       waits=[("MM", mm), ("LDA", lx)] + pw(xt_), inc="VE")
                setfree(pt, "VE", ve)
                setfree(xo, "VE", ve)
                vv = vfence()
                st_ = dma("sync", "STS",
                          x1_dram[oc * 128:(oc + 1) * 128, tsl], xt_,
                          waits=[("VE", vv)])
                x1_store[(oc, ts)] = st_
                sq = sqf[oc % 2]
                sc_ = E("scalar", lambda e, xt_=xt_, sq=sq:
                        e.activation(sq, xt_, AF.Square),
                        waits=[("VE", vv)] + pw(sq), inc="SC")
                sc2 = sfence()
                mm1 = E("tensor", lambda e, xt_=xt_, oc=oc:
                        e.matmul(P[6][0:1, :], onesc_f[:, 0:1], xt_,
                                 start=(oc == 0), stop=(oc == 15)),
                        waits=([("VE", vv)]
                               + (pw(P[6]) if oc == 0 else [])),
                        inc="MM")
                setfree(xt_, "MM", mm1)
                addfree(xt_, "SC", sc_)
                addfree(xt_, "STS", st_)
                mm2 = E("tensor", lambda e, sq=sq, oc=oc:
                        e.matmul(P[7][0:1, :], onesc_f[:, 0:1], sq,
                                 start=(oc == 0), stop=(oc == 15)),
                        waits=([("SC", sc2)]
                               + (pw(P[7]) if oc == 0 else [])),
                        inc="MM")
                setfree(sq, "MM", mm2)
            ve = E("vector", lambda e, ts=ts:
                   e.tensor_copy(nm_r[:, ts * 512:(ts + 1) * 512],
                                 P[6][0:1, :]),
                   waits=[("MM", p.cnt["MM"])], inc="VE")
            setfree(P[6], "VE", ve)
            ve = E("vector", lambda e, ts=ts:
                   e.tensor_copy(inv_r[:, ts * 512:(ts + 1) * 512],
                                 P[7][0:1, :]), inc="VE")
            setfree(P[7], "VE", ve)

        def ln_stats():
            E("vector", lambda e:
              e.tensor_scalar(nm_r, nm_r, -1.0 / DM, None, op0=ALU.mult))
            E("vector", lambda e:
              e.tensor_scalar(inv_r, inv_r, 1.0 / DM, None, op0=ALU.mult))
            E("vector", lambda e:
              e.tensor_tensor(tmp_r, nm_r, nm_r, op=ALU.mult))
            ve = E("vector", lambda e:
                   e.scalar_tensor_tensor(tmp_r, tmp_r, -1.0, inv_r,
                                          op0=ALU.mult, op1=ALU.add),
                   inc="VE")
            sc_ = E("scalar", lambda e:
                    e.activation(tmp_r, tmp_r, AF.Sqrt,
                                 bias=eps_s[0:1, 0:1], scale=1.0),
                    waits=[("VE", ve)], inc="SC")
            E("vector", lambda e:
              e.reciprocal(inv_r, tmp_r),
              waits=[("SC", sc_)], inc="VE")
            return vfence()

        ve_st1 = ln_stats()
        for ts in range(2):
            tsl = slice(ts * 512, (ts + 1) * 512)
            pnm, pinv = P[5], P[3]
            E("tensor", lambda e, pnm=pnm, ts=ts:
              e.matmul(pnm[:, :], onesr_f[0:1, :],
                       nm_r[:, ts * 512:(ts + 1) * 512],
                       start=True, stop=True),
              waits=[("VE", ve_st1)] + pw(pnm), inc="MM")
            mmb = E("tensor", lambda e, pinv=pinv, ts=ts:
                    e.matmul(pinv[:, :], onesr_f[0:1, :],
                             inv_r[:, ts * 512:(ts + 1) * 512],
                             start=True, stop=True),
                    waits=pw(pinv), inc="MM")
            for oc in range(KC):
                xt_ = x1t[oc % 2]
                lx = dma("scalar", "LDA", xt_,
                         x1_dram[oc * 128:(oc + 1) * 128, tsl],
                         waits=[("STS", x1_store[(oc, ts)])] + pw(xt_),
                         echo=True)
                xl = xlnt[oc % 2]
                E("vector", lambda e, xt_=xt_, xl=xl, pnm=pnm:
                  e.tensor_tensor(xl, xt_, pnm[:, :], op=ALU.add),
                  waits=[("LDA", lx), ("MM", mmb)] + pw(xl))
                vsep()
                ve = E("vector", lambda e, xl=xl, pinv=pinv:
                       e.tensor_tensor(xl, xl, pinv[:, :], op=ALU.mult),
                       inc="VE")
                setfree(xt_, "VE", ve)
                vv = vfence()
                st_ = dma("sync", "STS",
                          xln_dram[oc * 128:(oc + 1) * 128, tsl], xl,
                          waits=[("VE", vv)])
                E("vector", lambda e, xl=xl, oc=oc, tsl=tsl:
                  e.tensor_copy(xln16[oc][:, tsl], xl), inc="VE")
                pg = P[2] if ts == 0 else P[4]
                last_g = E("tensor",
                           lambda e, xl=xl, oc=oc, pg=pg:
                           e.matmul(pg[0:8, :],
                                    wg_s[:, oc * 8:(oc + 1) * 8],
                                    xl,
                                    start=(oc == 0), stop=(oc == 15)),
                           waits=([("VE", vv)]
                                  + (pw(pg) if oc == 0 else [])),
                           inc="MM")
                setfree(xl, "MM", last_g)
                addfree(xl, "STS", st_)
            setfree(pnm, "VE", p.cnt["VE"])
            setfree(pinv, "VE", p.cnt["VE"])
        sts_xln = p.cnt["STS"]
        mm_gate = p.cnt["MM"]
        for ts in range(2):
            pg = P[2] if ts == 0 else P[4]
            ve = E("vector", lambda e, ts=ts, pg=pg:
                   e.tensor_scalar(gt_sb[:, ts * 512:(ts + 1) * 512],
                                   pg[0:8, :], bg_s[0:8, 0:1], None,
                                   op0=ALU.add),
                   waits=[("MM", mm_gate), ("LDS", ld_consts)], inc="VE")
            setfree(pg, "VE", ve)
        ve_gt = vfence()
        for tt8 in range(8):
            pt2g = P[2 + (tt8 % 2)]
            mmg = E("tensor", lambda e, tt8=tt8, pt2g=pt2g:
                    e.transpose(pt2g[:, 0:8],
                                gt_sb[:, tt8 * 128:(tt8 + 1) * 128],
                                idf_s[0:8, 0:8]),
                    waits=[("VE", ve_gt), ("LDS", ld_consts)] + pw(pt2g),
                    inc="MM")
            ve = E("vector", lambda e, tt8=tt8, pt2g=pt2g:
                   e.tensor_copy(gate_s[:, tt8 * 8:(tt8 + 1) * 8],
                                 pt2g[:, 0:8]),
                   waits=[("MM", mmg)], inc="VE")
            setfree(pt2g, "VE", ve)

        GB = 64

        def gc(t, a, b):
            return scr[:, t * GB + a:t * GB + b]

        for t in range(8):
            E("vector", lambda e, t=t:
              e.tensor_scalar(gc(t, 0, 8), gate_s[:, t * 8:(t + 1) * 8],
                              -1.0, None, op0=ALU.mult))
        for t in range(8):
            E("vector", lambda e, t=t:
              e.reduce_max(gc(t, 48, 49), gc(t, 0, 8), axis=AX.X))
        for t in range(8):
            E("vector", lambda e, t=t:
              e.tensor_scalar(gc(t, 8, 16), gc(t, 0, 8), gc(t, 48, 49),
                              0.0, op0=ALU.subtract, op1=ALU.is_equal))
        for t in range(8):
            E("vector", lambda e, t=t:
              e.scalar_tensor_tensor(gc(t, 16, 24), gc(t, 8, 16), -1.0e30,
                                     gc(t, 0, 8), op0=ALU.mult, op1=ALU.add))
        for t in range(8):
            E("vector", lambda e, t=t:
              e.reduce_max(gc(t, 49, 50), gc(t, 16, 24), axis=AX.X))
        for t in range(8):
            E("vector", lambda e, t=t:
              e.tensor_scalar(gc(t, 24, 32), gc(t, 16, 24), gc(t, 49, 50),
                              0.0, op0=ALU.subtract, op1=ALU.is_equal))
        for t in range(8):
            E("vector", lambda e, t=t:
              e.tensor_tensor(gc(t, 32, 40), gc(t, 8, 16), gc(t, 24, 32),
                              op=ALU.add))
        for t in range(8):
            E("vector", lambda e, t=t:
              e.tensor_scalar(gc(t, 32, 40), gc(t, 32, 40), -1.0, 1.0,
                              op0=ALU.mult, op1=ALU.add))
        ve_nm = None
        for t in range(8):
            ve_nm = E("vector", lambda e, t=t:
                      e.reduce_max(gc(t, 50, 51),
                                   gate_s[:, t * 8:(t + 1) * 8],
                                   axis=AX.X, negate=True), inc="VE")
        sc_exp = None
        for t in range(8):
            sc_exp = E("scalar", lambda e, t=t:
                       e.activation(gc(t, 40, 48),
                                    gate_s[:, t * 8:(t + 1) * 8], AF.Exp,
                                    bias=gc(t, 50, 51), scale=1.0),
                       waits=[("VE", ve_nm)], inc="SC")
        for t in range(8):
            E("vector", lambda e, t=t:
              e.tensor_tensor(gc(t, 40, 48), gc(t, 40, 48), gc(t, 32, 40),
                              op=ALU.mult), waits=[("SC", sc_exp)])
        for t in range(8):
            E("vector", lambda e, t=t:
              e.reduce_sum(gc(t, 51, 52), gc(t, 40, 48), axis=AX.X))
        for t in range(8):
            E("vector", lambda e, t=t:
              e.reciprocal(gc(t, 52, 53), gc(t, 51, 52)))
        for t in range(8):
            E("vector", lambda e, t=t:
              e.tensor_scalar(w_s[:, t * 8:(t + 1) * 8], gc(t, 40, 48),
                              gc(t, 52, 53), None, op0=ALU.mult), inc="VE")

        for tt8 in range(8):
            cs = slice(tt8 * 8, tt8 * 8 + 8)
            E("vector", lambda e, cs=cs:
              e.tensor_copy(w128[:, 0:8], w_s[:, cs]),
              waits=pw(w128))
            ve2 = vfence()
            pt2 = P[2]
            mm = E("tensor", lambda e, pt2=pt2:
                   e.transpose(pt2[:, 0:128], w128[:, :], idf_s[:, :]),
                   waits=[("VE", ve2), ("LDS", ld_consts)] + pw(pt2),
                   inc="MM")
            setfree(w128, "MM", mm)
            ve3 = E("vector", lambda e, tt8=tt8, pt2=pt2:
                    e.tensor_copy(wT6[0:6, tt8 * 128:(tt8 + 1) * 128],
                                  pt2[0:6, 0:128]),
                    waits=[("MM", mm)], inc="VE")
            setfree(pt2, "VE", ve3)
        vsep()
        E("vector", lambda e: e.tensor_copy(wT6b[:, :], wT6[:, :]), inc="VE")
        ve_wT6 = vfence()
        setfree(scr, "SC", p.cnt["SC"])

        def hslot(slot, tc, hc):
            return hsf[slot * 16 + tc * 8 + hc]

        def sush(sc_i, tc):
            return hsf[sc_i * 2 + tc]

        state = {"mm_wave": None, "gbuf": 0}

        def gu_expert(n, slot):
            for tc in range(2):
                pb = P[4]
                mm = E("tensor", lambda e, pb=pb, n=n, tc=tc:
                       e.matmul(pb[:, :],
                                selh_s[0:6, n * 128:(n + 1) * 128],
                                wT6[0:6, tc * 512:(tc + 1) * 512],
                                start=True, stop=True),
                       waits=[("VE", ve_wT6), ("LDS", ld_consts)] + pw(pb),
                       inc="MM")
                ve = E("vector", lambda e, pb=pb, tc=tc:
                       e.tensor_copy(wBt[tc], pb[:, :]),
                       waits=[("MM", mm)] + pw(wBt[tc]), inc="VE")
                setfree(pb, "VE", ve)
            ve_wb = p.cnt["VE"]
            for q4 in range(4):
                hcs = (q4 * 2, q4 * 2 + 1)
                for which in ("g", "u"):
                    wt = Wg_img if which == "g" else Wu_img
                    gb = gslab[state["gbuf"] % 2]
                    state["gbuf"] += 1
                    lds = dma("gpsimd", "LDW", gb, wt[n, q4],
                              waits=pw(gb) + [("VE", ve_wT6)], echo=True)
                    for tc in range(2):
                        for hi, hc in enumerate(hcs):
                            pt = P[4 + ((tc * 2 + hi) % 4)]
                            mm = None
                            for kc in range(KC):
                                mm = E("tensor",
                                       lambda e, pt=pt, kc=kc, hi=hi, tc=tc,
                                       gb=gb:
                                       e.matmul(
                                           pt[:, :],
                                           gb[:, kc * 256 + hi * 128:
                                              kc * 256 + (hi + 1) * 128],
                                           xln16[kc][:,
                                                     tc * 512:(tc + 1) * 512],
                                           start=(kc == 0), stop=(kc == 15)),
                                       waits=([("LDW", lds),
                                               ("VE", ve_wT6)]
                                              + (pw(pt) if kc == 0 else [])),
                                       inc="MM" if kc == 15 else None)
                            if which == "g":
                                sgt = sg[tc * 2 + hi]
                                sc_ = E("scalar",
                                        lambda e, pt=pt, sgt=sgt, hc=hc, n=n:
                                        e.activation(
                                            sgt, pt[:, :], AF.Silu,
                                            bias=bgate_s[:, n * 8 + hc:
                                                         n * 8 + hc + 1],
                                            scale=1.0),
                                        waits=[("MM", mm)] + pw(sgt),
                                        inc="SC")
                                setfree(pt, "SC", sc_)
                                if tc == 1 and hi == 1:
                                    sfence()
                            else:
                                u_ = ut[hi]
                                sgt = sg[tc * 2 + hi]
                                ve = E("vector",
                                       lambda e, pt=pt, hc=hc, n=n, tc=tc,
                                       u_=u_:
                                       e.scalar_tensor_tensor(
                                           u_, pt[:, :],
                                           bup_s[:, n * 8 + hc:
                                                 n * 8 + hc + 1],
                                           wBt[tc],
                                           op0=ALU.add, op1=ALU.mult),
                                       waits=[("MM", mm), ("VE", ve_wb),
                                              ("SC", p.cnt["SC"])]
                                             + pw(u_), inc="VE")
                                setfree(pt, "VE", ve)
                                vsep()
                                hst = hslot(slot, tc, hc)
                                ve = E("vector",
                                       lambda e, hst=hst, sgt=sgt, u_=u_:
                                       e.tensor_tensor(hst, sgt, u_,
                                                       op=ALU.mult),
                                       waits=([("MM", state["mm_wave"])]
                                              if state["mm_wave"] else []),
                                       inc="VE")
                                setfree(u_, "VE", ve)
                                setfree(sgt, "VE", ve)
                    setfree(gb, "MM", p.cnt["MM"])

        def down_wave(wave, ns):
            ve_hs = vfence()
            for dg2 in range(8):
                for i, n in enumerate(ns):
                    rt = rtw[(dg2 * 2 + i) % 2]
                    ldr = dma("gpsimd", "LDW", rt, Wdn_img[dg2, n],
                              waits=pw(rt) + [("VE", ve_wT6)], echo=True)
                    mmlast = None
                    for dc in range(2):
                        for tc in range(2):
                            pdn = P[dc * 2 + tc]
                            slot = n % 2
                            for hc in range(HC):
                                st_mm = (i == 0 and hc == 0)
                                mmlast = E(
                                    "tensor",
                                    lambda e, pdn=pdn, rt=rt, dc=dc, tc=tc,
                                    hc=hc, slot=slot, st_mm=st_mm:
                                    e.matmul(
                                        pdn[:, :],
                                        rt[:, hc * 256 + dc * 128:
                                           hc * 256 + dc * 128 + 128],
                                        hslot(slot, tc, hc),
                                        start=(st_mm and wave == 0) or
                                              (st_mm and wave > 0),
                                        stop=False),
                                    waits=([("LDW", ldr), ("VE", ve_hs)]
                                           + (pw(pdn) if st_mm else [])),
                                    inc="MM" if hc == HC - 1 else None)
                    setfree(rt, "MM", mmlast)
                for dc in range(2):
                    dchunk = dg2 * 2 + dc
                    for tc in range(2):
                        pdn = P[dc * 2 + tc]
                        if wave == 0:
                            mm = E("tensor",
                                   lambda e, pdn=pdn, dchunk=dchunk, tc=tc:
                                   e.matmul(
                                       pdn[:, :],
                                       bdn_s[0:6,
                                             dchunk * 128:(dchunk + 1) * 128],
                                       wT6b[0:6, tc * 512:(tc + 1) * 512],
                                       start=False, stop=True),
                                   waits=[("VE", ve_wT6),
                                          ("LDW", ld_bdn)], inc="MM")
                            ve = E("vector",
                                   lambda e, pdn=pdn, dchunk=dchunk, tc=tc:
                                   e.tensor_copy(
                                       partial[dchunk][:,
                                                       tc * 512:
                                                       (tc + 1) * 512],
                                       pdn[:, :]),
                                   waits=[("MM", mm)], inc="VE")
                        else:
                            ve = E("vector",
                                   lambda e, pdn=pdn, dchunk=dchunk, tc=tc:
                                   e.tensor_tensor(
                                       partial[dchunk][:,
                                                       tc * 512:
                                                       (tc + 1) * 512],
                                       pdn[:, :],
                                       partial[dchunk][:,
                                                       tc * 512:
                                                       (tc + 1) * 512],
                                       op=ALU.add),
                                   waits=[("MM", p.cnt["MM"])], inc="VE")
                        setfree(pdn, "VE", ve)
            state["mm_wave"] = p.cnt["MM"]

        for wv, ns in enumerate(((0, 1), (2, 3), (4, 5))):
            for i, n in enumerate(ns):
                gu_expert(n, i)
            down_wave(wv, ns)

        sc_sup = None
        for q8 in range(8):
            gb = gslab[state["gbuf"] % 2]
            state["gbuf"] += 1
            lds = dma("gpsimd", "LDW", gb, Ws_img[q8],
                      waits=pw(gb) + [("VE", ve_wT6)], echo=True)
            for tc in range(2):
                for si in range(2):
                    sc_i = q8 * 2 + si
                    pt = P[4 + ((tc * 2 + si) % 4)]
                    mm = None
                    for kc in range(KC):
                        mm = E("tensor", lambda e, pt=pt, kc=kc, si=si,
                               tc=tc, gb=gb:
                               e.matmul(pt[:, :],
                                        gb[:, kc * 256 + si * 128:
                                           kc * 256 + (si + 1) * 128],
                                        xln16[kc][:,
                                                  tc * 512:(tc + 1) * 512],
                                        start=(kc == 0), stop=(kc == 15)),
                               waits=([("LDW", lds)]
                                      + (pw(pt) if kc == 0 else [])),
                               inc="MM" if kc == 15 else None)
                    sut = sush(sc_i, tc)
                    sc_sup = E("scalar",
                               lambda e, pt=pt, sut=sut, sc_i=sc_i:
                               e.activation(sut, pt[:, :], AF.Silu,
                                            bias=bsup_s[:, sc_i:sc_i + 1],
                                            scale=1.0),
                               waits=[("MM", mm)] + pw(sut), inc="SC")
                    setfree(pt, "SC", sc_sup)
                    setfree(sut, "SC", sc_sup)
            setfree(gb, "MM", p.cnt["MM"])
        sc_sup = sfence()

        x2_store = {}
        ve_waves = p.cnt["VE"]
        for dg2 in range(8):
            st0 = stw[dg2 % 2]
            lh1 = dma("gpsimd", "LDW", st0, Wsdn_img[dg2],
                      waits=pw(st0) + [("SC", sc_sup), ("VE", ve_waves)],
                      echo=True)
            for dc in range(2):
                dchunk = dg2 * 2 + dc
                for tc in range(2):
                    pdn = P[dc * 2 + tc]
                    mm = None
                    for sc_i in range(KC):
                        mm = E("tensor",
                               lambda e, pdn=pdn, sc_i=sc_i, dc=dc, tc=tc,
                               st0=st0:
                               e.matmul(
                                   pdn[:, :],
                                   st0[:, sc_i * 256 + dc * 128:
                                       sc_i * 256 + dc * 128 + 128],
                                   sush(sc_i, tc),
                                   start=(sc_i == 0), stop=(sc_i == 15)),
                               waits=([("LDW", lh1), ("SC", sc_sup)]
                                      + (pw(pdn) if sc_i == 0 else [])),
                               inc="MM" if sc_i == 15 else None)
                    tsl = slice(tc * 512, (tc + 1) * 512)
                    xl_ = xlt[dchunk % 2]
                    lx = dma("scalar", "LDA", xl_,
                             xln_dram[dchunk * 128:(dchunk + 1) * 128, tsl],
                             waits=pw(xl_) + [("STS", sts_xln)], echo=True)
                    x2_ = x2t[dchunk % 2]
                    E("vector", lambda e, pdn=pdn, dchunk=dchunk, tc=tc,
                      x2_=x2_:
                      e.scalar_tensor_tensor(
                          x2_, pdn[:, :], bsdn_s[:, dchunk:dchunk + 1],
                          partial[dchunk][:, tc * 512:(tc + 1) * 512],
                          op0=ALU.add, op1=ALU.add),
                      waits=[("MM", mm)] + pw(x2_))
                    vsep()
                    ve = E("vector", lambda e, x2_=x2_, xl_=xl_:
                           e.tensor_tensor(x2_, x2_, xl_, op=ALU.add),
                           waits=[("LDA", lx)], inc="VE")
                    setfree(pdn, "VE", ve)
                    setfree(xl_, "VE", ve)
                    vv = vfence()
                    st_ = dma("sync", "STS",
                              x2_dram[dchunk * 128:(dchunk + 1) * 128, tsl],
                              x2_, waits=[("VE", vv)])
                    x2_store[(dchunk, tc)] = st_
                    sq = sqf[dchunk % 2]
                    sc_ = E("scalar", lambda e, x2_=x2_, sq=sq:
                            e.activation(sq, x2_, AF.Square),
                            waits=[("VE", vv)] + pw(sq), inc="SC")
                    sc2 = sfence()
                    pssum = P[4 + tc]
                    pssq = P[6 + tc]
                    first = (dg2 == 0 and dc == 0)
                    mm1 = E("tensor", lambda e, x2_=x2_, pssum=pssum,
                            first=first:
                            e.matmul(pssum[0:1, :], onesc_f[:, 0:1],
                                     x2_, start=first, stop=False),
                            waits=([("VE", vv)]
                                   + (pw(pssum) if first else [])),
                            inc="MM")
                    setfree(x2_, "MM", mm1)
                    addfree(x2_, "SC", sc_)
                    addfree(x2_, "STS", st_)
                    mm2 = E("tensor", lambda e, sq=sq, pssq=pssq,
                            first=first:
                            e.matmul(pssq[0:1, :], onesc_f[:, 0:1],
                                     sq, start=first, stop=False),
                            waits=([("SC", sc2)]
                                   + (pw(pssq) if first else [])),
                            inc="MM")
                    setfree(sq, "MM", mm2)
            setfree(st0, "MM", p.cnt["MM"])
        mm_ln2 = p.cnt["MM"]
        for tc in range(2):
            ve = E("vector", lambda e, tc=tc:
                   e.tensor_copy(nm_r[:, tc * 512:(tc + 1) * 512],
                                 P[4 + tc][0:1, :]),
                   waits=[("MM", mm_ln2)], inc="VE")
            setfree(P[4 + tc], "VE", ve)
            ve = E("vector", lambda e, tc=tc:
                   e.tensor_copy(inv_r[:, tc * 512:(tc + 1) * 512],
                                 P[6 + tc][0:1, :]), inc="VE")
            setfree(P[6 + tc], "VE", ve)
        ve_st2 = ln_stats()
        sts_out = None
        for ts in range(2):
            tsl = slice(ts * 512, (ts + 1) * 512)
            pnm, pinv = P[5], P[3]
            E("tensor", lambda e, pnm=pnm, ts=ts:
              e.matmul(pnm[:, :], onesr_f[0:1, :],
                       nm_r[:, ts * 512:(ts + 1) * 512],
                       start=True, stop=True),
              waits=[("VE", ve_st2)] + pw(pnm), inc="MM")
            mmb = E("tensor", lambda e, pinv=pinv, ts=ts:
                    e.matmul(pinv[:, :], onesr_f[0:1, :],
                             inv_r[:, ts * 512:(ts + 1) * 512],
                             start=True, stop=True),
                    waits=pw(pinv), inc="MM")
            for oc in range(KC):
                x2_ = x2t[oc % 2]
                lx = dma("scalar", "LDA", x2_,
                         x2_dram[oc * 128:(oc + 1) * 128, tsl],
                         waits=[("STS", x2_store[(oc, ts)])] + pw(x2_),
                         echo=True)
                x3_ = x3t[oc % 2]
                E("vector", lambda e, x2_=x2_, x3_=x3_, pnm=pnm:
                  e.tensor_tensor(x3_, x2_, pnm[:, :], op=ALU.add),
                  waits=[("LDA", lx), ("MM", mmb)] + pw(x3_))
                vsep()
                ve = E("vector", lambda e, x3_=x3_, pinv=pinv:
                       e.tensor_tensor(x3_, x3_, pinv[:, :], op=ALU.mult),
                       inc="VE")
                setfree(x2_, "VE", ve)
                vv = vfence()
                sts_out = dma("sync", "STS",
                              out[oc * 128:(oc + 1) * 128, tsl],
                              x3_, waits=[("VE", vv)])
                setfree(x3_, "STS", sts_out)
            setfree(pnm, "VE", p.cnt["VE"])
            setfree(pinv, "VE", p.cnt["VE"])
        E("sync", lambda e: None, waits=[("STS", sts_out)])

        def runner(name):
            def run(eng):
                for need, fn, inc, n in p.steps[name]:
                    for s, v in need:
                        eng.wait_ge(sem[s], v)
                    if fn is None:
                        continue
                    inst = fn(eng)
                    if inc is not None:
                        inst.then_inc(sem[inc], n)
            return run

        block.sync(runner("sync"))
        block.gpsimd(runner("gpsimd"))
        block.tensor(runner("tensor"))
        block.vector(runner("vector"))
        block.scalar(runner("scalar"))

    return nc


_NC_CACHE = {}


def _build_cached():
    if "nc" not in _NC_CACHE:
        _NC_CACHE["nc"] = build_nc()
    return _NC_CACHE["nc"]


def _rope_tables():
    inv = 1.0 / (10000.0 ** (np.arange(0, HD, 2, dtype=np.float64) / HD))
    t = np.arange(S, dtype=np.float64)
    fr = t[:, None] * inv[None, :]
    emb = np.concatenate([fr, fr], axis=1)
    cos = np.cos(emb).T.astype(np.float32)
    sin = np.sin(emb).T.astype(np.float32)
    return cos, sin


def _cols(vec, n):
    return np.ascontiguousarray(
        np.asarray(vec).reshape(n, 128).T.astype(np.float32))


def kernel(**inputs):
    f = {k: np.asarray(v) for k, v in inputs.items()}
    x = f["L0_attn_hidden_states"].astype(np.float32, copy=False)
    wq, bq = f["L0_attn_wq"], f["L0_attn_bq"]
    wk, bk = f["L0_attn_wk"], f["L0_attn_bk"]
    wv, bv = f["L0_attn_wv"], f["L0_attn_bv"]
    wo, bo = f["L0_attn_wo"], f["L0_attn_bo"]

    cos, sin = _rope_tables()
    cosq = (cos * SCALE).astype(BF)
    sinq = (sin * SCALE).astype(BF)
    cosk = cos.astype(BF)
    sink = sin.astype(BF)
    rsT = np.zeros((128, 128), dtype=np.float32)
    for i_ in range(64):
        rsT[64 + i_, i_] = -1.0
        rsT[i_, 64 + i_] = 1.0
    rsT = rsT.astype(BF)
    dm = np.zeros((128, 4 * 512), dtype=np.float32)
    ii = np.arange(128)[:, None]
    jj = np.arange(512)[None, :]
    for m in range(4):
        dm[:, m * 512:(m + 1) * 512] = (m * 128 + ii <= jj)
    dmask = dm.astype(BF)
    idf = np.eye(128, dtype=np.float32)
    selh = np.zeros((NR, NR * 128), dtype=np.float32)
    for n in range(NR):
        selh[n, n * 128:(n + 1) * 128] = 1.0

    woT = np.ascontiguousarray(wo.T)

    def img16(w2d, qw):
        m = w2d.shape[1]
        return np.ascontiguousarray(
            w2d.reshape(KC, 128, m // qw, qw).transpose(2, 1, 0, 3).reshape(
                m // qw, 128, KC * qw).astype(np.float32))

    Wdn_img = np.ascontiguousarray(
        f["L0_ffn_Wdown"].reshape(NR, HC, 128, 8, 256).transpose(
            3, 0, 2, 1, 4).reshape(8, NR, 128, 2048).astype(np.float32))
    Wsdn_img = img16(f["L0_ffn_Ws_down"], 256)
    Wg2 = np.ascontiguousarray(
        f["L0_ffn_Wg"].reshape(KC, 128, 8).transpose(1, 0, 2).reshape(128, 128))

    common = dict(
        Wg2=Wg2, bgr=f["L0_ffn_bg"].reshape(8, 1).astype(np.float32),
        Wg_img=np.stack([img16(f["L0_ffn_Wgate"][n], 256)
                         for n in range(NR)]),
        Wu_img=np.stack([img16(f["L0_ffn_Wup"][n], 256)
                         for n in range(NR)]),
        bgatec=_cols(f["L0_ffn_bgate"].reshape(-1), NR * HC),
        bupc=_cols(f["L0_ffn_bup"].reshape(-1), NR * HC),
        Wdn_img=Wdn_img, bdn=f["L0_ffn_bdown"].astype(np.float32),
        Ws_img=img16(f["L0_ffn_Ws_up"].astype(np.float32), 256),
        bsupc=_cols(f["L0_ffn_bs_up"], KC),
        Wsdn_img=Wsdn_img, bsdnc=_cols(f["L0_ffn_bs_down"], KC),
        boc=_cols(bo, KC),
        cosq=cosq, sinq=sinq, cosk=cosk, sink=sink, dmask=dmask, idf=idf,
        selh=selh, rsT=rsT,
    )

    in_maps = []
    for c in range(8):
        b, par = c // 2, c % 2
        hsl = slice(par * 1024, (par + 1) * 1024)
        psl = slice((1 - par) * 1024, (2 - par) * 1024)
        sel = np.zeros((128, 4), dtype=np.float32)
        if par == 0:
            sel[:, 1] = 1.0
            sel[:, 2] = 1.0
        else:
            sel[:, 0] = 1.0
            sel[:, 3] = 1.0
        m = dict(common)
        m.update(
            xT=np.ascontiguousarray(x[b].T),
            xownT=np.ascontiguousarray(x[b, hsl].T),
            wq_img=img16(np.ascontiguousarray(wq[hsl].T.astype(np.float32)), 256),
            wk_img=img16(np.ascontiguousarray(wk[hsl].T.astype(np.float32)), 256),
            wv_img=img16(np.ascontiguousarray(wv[hsl].T.astype(np.float32)), 256),
            woTl=np.ascontiguousarray(
                np.concatenate([woT[hsl], woT[psl]], axis=0).astype(
                    np.float32)),
            bqc=_cols(bq[hsl], NHL), bkc=_cols(bk[hsl], NHL),
            bvr=np.asarray(bv[hsl]).reshape(1, -1).astype(np.float32),
            selc=sel,
        )
        in_maps.append(m)

    nc = _build_cached()
    res = run_bass_kernel_spmd(nc, in_maps, core_ids=list(range(8)))

    y = np.empty((B, S, DM), dtype=np.float32)
    for c in range(8):
        b, par = c // 2, c % 2
        y[b, par * 1024:(par + 1) * 1024, :] = res.results[c]["out"].T
    return y
